# version 07
# baseline (speedup 1.0000x reference)
import numpy as np
import ml_dtypes
from contextlib import ExitStack
import concourse.bass as bass
import concourse.mybir as mybir
from concourse.bass_utils import run_bass_kernel_spmd

F32 = mybir.dt.float32
BF16 = mybir.dt.bfloat16
AF = mybir.ActivationFunctionType
ALU = mybir.AluOpType
AX = mybir.AxisListType

S, D = 4096, 2048
CH, NCH, TS = 512, 8, 4
KD = D // 128
E, DE, NDE = 8, 1408, 11
DSH, NDS = 2816, 22
F0 = 6144
NH, DH, ROT = 16, 128, 32
HF, NHF = 5120, 40
LN_EPS = 1.0e-5
RS128 = 0.08838834764831845

DEV_DUMP = False
DEV_DUMP_CHUNK = 0
PAIRS = [(i, j) for i in range(16) for j in range(i + 1)]
POFF = [i * (i + 1) // 2 for i in range(16)]


def _mw_layout():
    off, o = {}, 0
    off["wqkv0"] = o; o += 12 * 128 * KD * 512
    off["wout0"] = o; o += 16 * 128 * KD * 128
    off["router"] = o; o += 128 * KD * 8
    off["gate"] = o; o += E * NDE * 128 * KD * 128
    off["up"] = o; o += E * NDE * 128 * KD * 128
    off["down"] = o; o += E * 16 * 128 * NDE * 128
    off["sg"] = o; o += NDS * 128 * KD * 128
    off["su"] = o; o += NDS * 128 * KD * 128
    off["sd"] = o; o += 16 * 128 * NDS * 128
    return off, o


MW_OFF, MW_TOTAL = _mw_layout()
Q01FC_TOTAL = NHF * 128 * KD * 128
Q01PJ_TOTAL = KD * 128 * NHF * 128


def _ap(t, off, dims):
    return bass.AP(t.tensor if isinstance(t, bass.AP) else t, off, dims)


class Ctx:
    def __init__(self, nc, stack):
        self.nc, self.stack, self.h, self.c = nc, stack, {}, {}

    def sem(self, name):
        if name not in self.h:
            self.h[name] = self.stack.enter_context(self.nc.semaphore(name))
            self.c[name] = 0
        return self.h[name]

    def inc(self, instr, name, n=1):
        instr.then_inc(self.sem(name), n)
        self.c[name] += n
        return self.c[name]

    def w(self, eng, name, val):
        eng.wait_ge(self.sem(name), val)

    def wall(self, eng, name):
        eng.wait_ge(self.sem(name), self.c[name])

    def val(self, name):
        self.sem(name)
        return self.c[name]

    def fence(self, eng, ins, tag):
        n = self.inc(ins, tag)
        eng.wait_ge(self.h[tag], n)


def _clear_sems(nc, cx):
    with nc.Block() as blk:
        @blk.sync
        def _(sy):
            for nm in list(cx.h):
                sy.sem_clear(cx.h[nm])


def _emit_ident(nc, cx, ident, iota_scr):
    with nc.Block() as blk:
        @blk.gpsimd
        def _(g):
            ins = g.iota(iota_scr[:, :], [[1, 128]], base=0, channel_multiplier=-1,
                         allow_small_or_imprecise_dtypes=True)
            cx.inc(ins, "id_g", 1)

        @blk.vector
        def _(v):
            cx.w(v, "id_g", cx.val("id_g"))
            ins = v.tensor_single_scalar(ident[:, :], iota_scr[:, :], 0.0, ALU.is_equal)
            cx.inc(ins, "id_v", 1)


PREP_PARTS = {"dma", "stats", "scal", "qfc", "qpj"}


def emit_prep(nc, T, cx=None):
    with ExitStack() as st:
        own_cx = cx is None
        cx = cx if cx is not None else Ctx(nc, st)
        sb = lambda nm, shp, dt: st.enter_context(nc.sbuf_tensor(nm, shp, dt))
        mw = T["g_mw"]

        wstat = sb("pp_wstat", [128, 64], F32)
        wtile = sb("pp_wtile", [128, 2 * HF], F32)
        sc1 = sb("pp_sc1", [1, 16], F32)
        scb = sb("pp_scb", [128, 4], F32)
        ones_r = sb("pp_ones", [1, 128], F32)
        ones_rb = sb("pp_onesb", [1, 128], BF16)
        ones_c = sb("pp_onec", [128, 1], F32)
        ident = sb("pp_id", [128, 128], BF16)
        iota_s = sb("pp_io", [128, 128], F32)
        qsgn = sb("pp_sgn", [128, 2 * HF], BF16)
        qabs = sb("pp_abs", [128, 2 * HF], F32)
        qm01 = sb("pp_m01", [128, HF], BF16)
        q01 = sb("pp_q01", [128, HF], BF16)
        qrow = sb("pp_qrow", [128, HF], BF16)
        pt = [st.enter_context(nc.psum_tensor(f"pp_pt{i}", [128, 128], BF16))
              for i in range(2)]
        ps = st.enter_context(nc.psum_tensor("pp_ps", [128, 4], F32))
        pp_cb = st.enter_context(nc.psum_tensor("pp_cb", [128, 16], F32))

        _emit_ident(nc, cx, ident, iota_s)

        with nc.Block() as blk:
            @blk.gpsimd
            def _(g):
                g.memset(ones_r[:, :], 1.0)
                g.memset(ones_rb[:, :], 1.0)
                g.memset(ones_c[:, :], 1.0)
                g.memset(qm01[64:80, 384:385], 1.0)
                cx.inc(g.dma_start(out=qm01[64:80, 0:384],
                                   in_=T["bqkv0"][None, :].rearrange(
                                       "o (h f) -> (o h) f", f=384)), "bql", 16)
                cx.inc(g.dma_start(out=qm01[32:48, 0:128],
                                   in_=_ap(T["bout0"], 0, [[128, 16], [1, 128]])),
                       "bql", 16)
                if "dma" not in PREP_PARTS:
                    return
                for fc in range(12):
                    dst = _ap(mw, MW_OFF["wqkv0"] + fc * 128 * KD * 512,
                              [[KD * 512, 128], [512, KD], [1, 512]])
                    src = _ap(T["Wqkv0"], fc * 512, [[F0, 128], [128 * F0, KD], [1, 512]])
                    cx.inc(g.dma_start(out=dst, in_=src), "pdma", 16)
                for mi in range(16):
                    dst = _ap(mw, MW_OFF["wout0"] + mi * 128 * KD * 128,
                              [[KD * 128, 128], [128, KD], [1, 128]])
                    src = _ap(T["Wout0"], mi * 128, [[D, 128], [128 * D, KD], [1, 128]])
                    cx.inc(g.dma_start(out=dst, in_=src), "pdma", 16)
                dst = _ap(mw, MW_OFF["router"], [[KD * 8, 128], [8, KD], [1, 8]])
                src = _ap(T["Wr"], 0, [[8, 128], [128 * 8, KD], [1, 8]])
                cx.inc(g.dma_start(out=dst, in_=src), "pdma", 16)
                for nm, wt in (("gate", T["Weg"]), ("up", T["Weu"])):
                    for e in range(E):
                        for mi in range(NDE):
                            dst = _ap(mw, MW_OFF[nm] + (e * NDE + mi) * 128 * KD * 128,
                                      [[KD * 128, 128], [128, KD], [1, 128]])
                            src = _ap(wt, e * D * DE + mi * 128,
                                      [[DE, 128], [128 * DE, KD], [1, 128]])
                            cx.inc(g.dma_start(out=dst, in_=src), "pdma", 16)
                for e in range(E):
                    for mi in range(16):
                        dst = _ap(mw, MW_OFF["down"] + (e * 16 + mi) * 128 * NDE * 128,
                                  [[NDE * 128, 128], [128, NDE], [1, 128]])
                        src = _ap(T["Wed"], e * DE * D + mi * 128,
                                  [[D, 128], [128 * D, NDE], [1, 128]])
                        cx.inc(g.dma_start(out=dst, in_=src), "pdma", 16)
                for nm, wt in (("sg", T["Wsg"]), ("su", T["Wsu"])):
                    for mi in range(NDS):
                        dst = _ap(mw, MW_OFF[nm] + mi * 128 * KD * 128,
                                  [[KD * 128, 128], [128, KD], [1, 128]])
                        src = _ap(wt, mi * 128, [[DSH, 128], [128 * DSH, KD], [1, 128]])
                        cx.inc(g.dma_start(out=dst, in_=src), "pdma", 16)
                for mi in range(16):
                    dst = _ap(mw, MW_OFF["sd"] + mi * 128 * NDS * 128,
                              [[NDS * 128, 128], [128, NDS], [1, 128]])
                    src = _ap(T["Wsd"], mi * 128, [[D, 128], [128 * D, NDS], [1, 128]])
                    cx.inc(g.dma_start(out=dst, in_=src), "pdma", 16)

        with nc.Block() as blk:
            @blk.sync
            def _(sy):
                if "stats" not in PREP_PARTS:
                    return
                for r in range(NHF):
                    if r >= 2:
                        cx.w(sy, "sred", r - 1)
                    cx.inc(sy.dma_start(out=wtile[:, (r % 2) * D:(r % 2) * D + D],
                                        in_=_ap(T["wfc"], r * 128 * D, [[D, 128], [1, D]])),
                           "sdma", 16)
                for r2 in range(KD):
                    cx.w(sy, "sred", NHF + max(r2 - 1, 0))
                    cx.inc(sy.dma_start(out=wtile[:, (r2 % 2) * HF:(r2 % 2) * HF + HF],
                                        in_=_ap(T["wproj"], r2 * 128 * HF,
                                                [[HF, 128], [1, HF]])), "sdma", 16)

            @blk.vector
            def _(v):
                if "stats" not in PREP_PARTS:
                    return
                for r in range(NHF):
                    cx.w(v, "sdma", 16 * (r + 1))
                    ins = v.reduce_sum(wstat[:, r:r + 1],
                                       wtile[:, (r % 2) * D:(r % 2) * D + D],
                                       AX.X, apply_absolute_value=True)
                    cx.inc(ins, "sred", 1)
                for r2 in range(KD):
                    cx.w(v, "sdma", 16 * (NHF + r2 + 1))
                    ins = v.reduce_sum(wstat[:, 40 + r2:41 + r2],
                                       wtile[:, (r2 % 2) * HF:(r2 % 2) * HF + HF],
                                       AX.X, apply_absolute_value=True)
                    cx.inc(ins, "sred", 1)
                cx.wall(v, "sred")
                v.reduce_sum(wstat[:, 60:61], wstat[:, 0:NHF], AX.X)
                ins = v.reduce_sum(wstat[:, 61:62], wstat[:, 40:40 + KD], AX.X)
                cx.inc(ins, "svec", 1)

        if "scal" not in PREP_PARTS:
            return
        b2 = {"svec": cx.val("svec")}
        with nc.Block() as blk:
            @blk.sync
            def _(sy):
                cx.inc(sy.dma_start(out=_ap(T["g_tmp"], 0, [[1, 128], [128, 2]]),
                                    in_=wstat[:, 60:62],
                                    allow_slow_non_contiguous=True), "bnc", 16)
                cx.w(sy, "bnc", 16)
                cx.inc(sy.dma_start(out=wtile[0:1, 0:256],
                                    in_=_ap(T["g_tmp"], 0, [[256, 1], [1, 256]])),
                       "bnc", 16)

            @blk.vector
            def _(v):
                cx.w(v, "bnc", 32)
                v.reduce_sum(sc1[0:1, 5:6], wtile[0:1, 0:128], AX.X)
                ins = v.reduce_sum(sc1[0:1, 6:7], wtile[0:1, 128:256], AX.X)
                cx.fence(v, ins, "fv")
                v.tensor_scalar(sc1[0:1, 0:1], sc1[0:1, 5:6], 1.0 / (HF * D), 1.0e-5,
                                ALU.mult, ALU.max)
                ins = v.tensor_scalar(sc1[0:1, 1:2], sc1[0:1, 6:7], 1.0 / (HF * D),
                                      1.0e-5, ALU.mult, ALU.max)
                cx.fence(v, ins, "fv")
                ins = v.tensor_scalar_mul(sc1[0:1, 2:4], sc1[0:1, 0:2], 0.7)
                ins = v.tensor_mul(sc1[0:1, 4:5], sc1[0:1, 0:1], sc1[0:1, 0:1])
                cx.fence(v, ins, "fv")
                ins = v.tensor_mul(sc1[0:1, 4:5], sc1[0:1, 4:5], sc1[0:1, 1:2])
                cx.inc(ins, "svec", 1)

            @blk.scalar
            def _(s):
                cx.w(s, "svec", b2["svec"] + 1)
                ins = s.activation(qm01[0:1, 0:3], sc1[0:1, 2:5], AF.Copy)
                cx.inc(ins, "shl", 1)

            @blk.vector
            def _(v):
                cx.w(v, "shl", 1)
                ins = v.tensor_copy(sc1[0:1, 8:11], qm01[0:1, 0:3])
                cx.fence(v, ins, "fv")
                ins = v.tensor_sub(sc1[0:1, 11:14], sc1[0:1, 2:5], sc1[0:1, 8:11])
                cx.fence(v, ins, "fv")
                ins = v.tensor_copy(qm01[0:1, 4:7], sc1[0:1, 11:14])
                cx.inc(ins, "svec", 1)

            @blk.tensor
            def _(t):
                cx.w(t, "id_v", cx.val("id_v"))
                cx.w(t, "svec", b2["svec"] + 2)
                t.matmul(ps[:, 0:3], ones_rb[0:1, :], qm01[0:1, 0:3],
                         start=True, stop=False)
                ins = t.matmul(ps[:, 0:3], ones_rb[0:1, :], qm01[0:1, 4:7],
                               start=False, stop=True)
                cx.inc(ins, "pmm", 1)

            @blk.vector
            def _(v):
                cx.w(v, "pmm", 1)
                cx.inc(v.tensor_copy(scb[:, 0:3], ps[:, 0:3]), "svec", 1)

            @blk.sync
            def _(sy):
                cx.w(sy, "svec", cx.val("svec"))
                cx.inc(sy.dma_start(out=T["g_scl"][0:1, 0:1], in_=scb[0:1, 2:3]),
                       "odma", 16)

        vacc = wtile
        with nc.Block() as blk:
            @blk.sync
            def _(sy):
                for h in range(16):
                    if h >= 2:
                        cx.w(sy, "vac", h - 1)
                    cx.inc(sy.dma_start(
                        out=vacc[:, 2 * D + (h % 2) * D:2 * D + (h % 2) * D + D],
                        in_=_ap(T["Wqkv0"], h * 384 + 256,
                                [[F0, 128], [128 * F0, KD], [1, 128]])), "vld", 16)
                for h in range(16):
                    cx.w(sy, "vac", (16 + h) if h >= 2 else 17)
                    cx.inc(sy.dma_start(
                        out=vacc[:, 2 * D + (h % 2) * D:2 * D + (h % 2) * D + D],
                        in_=_ap(T["Wout0"], h * 128 * D, [[D, 128], [1, D]])),
                        "vld", 16)

            @blk.vector
            def _(v):
                for h in range(16):
                    cx.w(v, "vld", 16 * (h + 1))
                    if h == 0:
                        ins = v.tensor_copy(vacc[:, 0:D],
                                            vacc[:, 2 * D:3 * D])
                    else:
                        ins = v.tensor_add(vacc[:, 0:D], vacc[:, 0:D],
                                           vacc[:, 2 * D + (h % 2) * D:
                                                2 * D + (h % 2) * D + D])
                    cx.inc(ins, "vac", 1)
                    cx.wall(v, "vac")
                ins = v.tensor_copy(qrow[:, 0:D], vacc[:, 0:D])
                cx.inc(ins, "vac", 1)
                for h in range(16):
                    cx.w(v, "vld", 16 * (16 + h + 1))
                    if h == 0:
                        ins = v.tensor_copy(vacc[:, D:2 * D], vacc[:, 2 * D:3 * D])
                    else:
                        ins = v.tensor_add(vacc[:, D:2 * D], vacc[:, D:2 * D],
                                           vacc[:, 2 * D + (h % 2) * D:
                                                2 * D + (h % 2) * D + D])
                    cx.inc(ins, "vac", 1)
                    cx.wall(v, "vac")
                ins = v.tensor_copy(qrow[:, D:2 * D], vacc[:, D:2 * D])
                cx.inc(ins, "wsb", 1)

            @blk.sync
            def _(sy):
                cx.w(sy, "vac", 17 + 16)
                cx.inc(sy.dma_start(out=_ap(T["g_wv"], 0, [[D, 128], [1, D]]),
                                    in_=qrow[:, 0:D]), "vout", 16)
                cx.w(sy, "wsb", 1)
                cx.inc(sy.dma_start(out=_ap(T["g_ws"], 0, [[D, 128], [1, D]]),
                                    in_=qrow[:, D:2 * D]), "vout", 16)


        with nc.Block() as blk:
            @blk.tensor
            def _(t):
                cx.w(t, "bql", 32)
                cx.w(t, "id_v", cx.val("id_v"))
                cx.w(t, "wsb", 1)
                ins = t.matmul(ps[:, 0:1], qm01[64:80, 256:384],
                               qm01[64:80, 384:385], start=True, stop=True)
                cx.inc(ins, "bvp", 1)
                ins = t.transpose(pt[1][:, 0:16], qm01[32:48, 0:128],
                                  ident[32:48, 32:48])
                cx.inc(ins, "bvp", 1)

            @blk.vector
            def _(v):
                cx.w(v, "bvp", 2)
                ins = v.tensor_copy(q01[:, 0:1], ps[:, 0:1])
                cx.inc(ins, "bvv", 1)

            @blk.tensor
            def _(t):
                cx.w(t, "bvv", 1)
                for mi in range(16):
                    ins = t.matmul(ps[:, 0:1] if False else pp_cb[:, mi:mi + 1],
                                   qrow[:, D + mi * 128:D + mi * 128 + 128],
                                   q01[:, 0:1], start=True, stop=True)
                cx.inc(ins, "cbp", 1)

            @blk.vector
            def _(v):
                cx.w(v, "cbp", 1)
                ins = v.tensor_copy(wstat[:, 0:16], pt[1][:, 0:16])
                cx.fence(v, ins, "fv")
                ins = v.tensor_add(wstat[:, 16:32], pp_cb[:, 0:16], wstat[:, 0:16])
                cx.inc(ins, "cbv", 1)

            @blk.sync
            def _(sy):
                cx.w(sy, "cbv", 1)
                cx.inc(sy.dma_start(out=T["g_cb"][:, :], in_=wstat[:, 16:32]),
                       "vout", 16)
                cx.wall(sy, "vout")

        def q_pass(which):
            if which == "fc":
                ntile, width, thr_col, noutk = NHF, D, 0, KD
                gdst, src = T["g_q01fc"], T["wfc"]
            else:
                ntile, width, thr_col, noutk = KD, HF, 1, NHF
                gdst, src = T["g_q01pj"], T["wproj"]
            qd, qs, qv, qt, qe, qo = [x + which for x in
                                      ("qd", "qs", "qv", "qt", "qe", "qo")]
            with nc.Block() as blk:
                @blk.sync
                def _(sy):
                    for r in range(ntile + 3):
                        if r < ntile:
                            if r >= 2:
                                cx.w(sy, qs, 2 * (r - 1))
                            cx.inc(sy.dma_start(
                                out=wtile[:, (r % 2) * width:(r % 2) * width + width],
                                in_=_ap(src, r * 128 * width,
                                        [[width, 128], [1, width]])), qd, 16)
                        if r >= 3:
                            r2 = r - 3
                            cx.w(sy, qe, (r2 + 1) * noutk)
                            cx.inc(sy.dma_start(
                                out=_ap(gdst, r2 * 128 * noutk * 128,
                                        [[noutk * 128, 128], [1, noutk * 128]]),
                                in_=qrow[:, 0:noutk * 128]), qo, 16)

                @blk.scalar
                def _(s):
                    for r in range(ntile):
                        cx.w(s, qd, 16 * (r + 1))
                        if r >= 2:
                            cx.w(s, qv, r - 1)
                        wt = wtile[:, (r % 2) * width:(r % 2) * width + width]
                        sl = (r % 2) * HF
                        ins = s.activation(qsgn[:, sl:sl + width], wt, AF.Sign)
                        cx.inc(ins, qs, 1)
                        ins = s.activation(qabs[:, sl:sl + width], wt, AF.Abs)
                        cx.inc(ins, qs, 1)

                @blk.vector
                def _(v):
                    def evacs(r):
                        for cb in range(noutk):
                            cx.w(v, qt, r * noutk + cb + 1)
                            ins = v.tensor_copy(qrow[:, cb * 128:cb * 128 + 128],
                                                pt[cb % 2][:, :])
                            cx.inc(ins, qe, 1)
                    for r in range(ntile):
                        cx.w(v, qs, 2 * (r + 1))
                        sl = (r % 2) * HF
                        ins = v.tensor_single_scalar(qm01[:, 0:width],
                                                     qabs[:, sl:sl + width],
                                                     scb[:, thr_col:thr_col + 1],
                                                     ALU.is_gt)
                        cx.fence(v, ins, "fv")
                        if r >= 1:
                            cx.w(v, qt, r * noutk)
                        ins = v.tensor_mul(q01[:, 0:width], qm01[:, 0:width],
                                           qsgn[:, sl:sl + width])
                        cx.inc(ins, qv, 1)
                        evacs(r)

                @blk.tensor
                def _(t):
                    for r in range(ntile):
                        cx.w(t, qv, r + 1)
                        if r >= 1:
                            cx.w(t, qo, 16 * r)
                        for cb in range(noutk):
                            if r * noutk + cb >= 2:
                                cx.w(t, qe, r * noutk + cb - 1)
                            ins = t.transpose(pt[cb % 2][:, :],
                                              q01[:, cb * 128:cb * 128 + 128],
                                              ident[:, :])
                            cx.inc(ins, qt, 1)

        if "qfc" in PREP_PARTS:
            q_pass("fc")
        if "qpj" in PREP_PARTS:
            q_pass("pj")

        with nc.Block() as blk:
            @blk.sync
            def _(sy):
                cx.wall(sy, "pdma")
                cx.wall(sy, "odma")
                cx.wall(sy, "vout")
                cx.wall(sy, "qofc")
                cx.wall(sy, "qopj")
        if own_cx:
            _clear_sems(nc, cx)


def _prep_end():
    pass


def emit_p1(nc, T, cx=None):
    with ExitStack() as st:
        own_cx = cx is None
        cx = cx if cx is not None else Ctx(nc, st)
        sb = lambda nm, shp, dt: st.enter_context(nc.sbuf_tensor(nm, shp, dt))
        mw = T["g_mw"]

        ident = sb("p1_id", [128, 128], BF16)
        identf = sb("p1_idf", [128, 128], F32)
        iota_s = sb("p1_io", [128, 128], F32)
        ones_r = sb("p1_on", [1, 512], BF16)
        router_w = sb("p1_rw", [128, KD * 8], BF16)
        wv_sb = sb("p1_wv", [128, KD * 128], BF16)
        ws_sb = sb("p1_ws", [128, D], BF16)
        c_pt = sb("p1_cpt", [128, 16], F32)
        lnst = sb("p1_ln", [128, 16], F32)
        cst = sb("p1_cst8", [128, 8], F32)
        x_tile = sb("p1_x", [128, 2 * D], F32)
        xn_tm = sb("p1_xntm", [128, 2 * D], BF16)
        xn_fm = sb("p1_xnfm", [128, KD * CH], BF16)
        vs_t = sb("p1_vst", [128, 2 * 128], BF16)
        vsT = sb("p1_vsT", [128, 2 * 128], BF16)
        cstat = sb("p1_cst", [128, 24], F32)
        s_fm = sb("p1_sfm", [128, KD * CH], BF16)
        comb_tm = sb("p1_ctm", [128, TS * 8], F32)
        comb_T = sb("p1_cT", [8, CH], F32)
        comb_bc = sb("p1_cbc", [128, E * CH], BF16)
        h_buf = sb("p1_h", [128, NDE * CH], BF16)
        racc = sb("p1_ra", [128, KD * CH], F32)
        g_fm = xn_fm
        silu_t = sb("p1_si", [128, 2 * CH], BF16)
        h_t = sb("p1_ht", [128, 2 * CH], BF16)
        sg_tm = sb("p1_sg", [128, D], BF16)
        xn2f_t = sb("p1_x2f", [128, D], BF16)
        NSL, SLW = 3, 4096
        moe_w = sb("p1_wm", [128, NSL * SLW], BF16)
        comb_rows = h_buf

        p_a = [st.enter_context(nc.psum_tensor(f"p1_pa{i}", [128, 512], F32))
               for i in range(4)]
        p_t = [st.enter_context(nc.psum_tensor(f"p1_pt{i}", [128, 128], BF16))
               for i in range(2)]
        p_r8 = st.enter_context(nc.psum_tensor("p1_pr8", [128, 8], F32))
        p_c8 = st.enter_context(nc.psum_tensor("p1_pc8", [8, 512], F32))

        _emit_ident(nc, cx, ident, iota_s)
        with nc.Block() as blk:
            @blk.vector
            def _(v):
                cx.w(v, "id_v", cx.val("id_v"))
                ins = v.tensor_single_scalar(identf[:, :], iota_s[:, :], 0.0,
                                             ALU.is_equal)
                cx.inc(ins, "id_v", 1)

        with nc.Block() as blk:
            @blk.gpsimd
            def _(g):
                g.memset(ones_r[:, :], 1.0)
                g.memset(cst[:, 0:1], LN_EPS)
                g.memset(cst[:, 1:2], -0.5)
                g.memset(cst[:, 2:3], 1.0 / D)
                ins = g.memset(cst[:, 3:4], -1.0)
                cx.inc(ins, "ldm", 1)

            @blk.sync
            def _(sy):
                cx.inc(sy.dma_start(out=router_w[:, :],
                                    in_=_ap(mw, MW_OFF["router"],
                                            [[KD * 8, 128], [1, KD * 8]])),
                       "ldh", 16)
                cx.inc(sy.dma_start(out=wv_sb[:, :],
                                    in_=_ap(T["g_wv"], 0, [[D, 128], [1, D]])),
                       "ldh", 16)
                cx.inc(sy.dma_start(out=ws_sb[:, :],
                                    in_=_ap(T["g_ws"], 0, [[D, 128], [1, D]])),
                       "ldh", 16)
                cx.inc(sy.dma_start(out=c_pt[:, :], in_=T["g_cb"][:, :]),
                       "ldh", 16)

        def ln_scalar(s, x_in, xn_out, junk_bf, tag):
            s.activation(junk_bf, x_in, AF.Copy, accum_out=lnst[:, 0:1])
            ins = s.activation(junk_bf, x_in, AF.Square, accum_out=lnst[:, 1:2])
            cx.fence(s, ins, "fs")
            s.mul(lnst[:, 2:3], lnst[:, 0:1], cst[:, 2:3])
            ins = s.mul(lnst[:, 3:4], lnst[:, 1:2], cst[:, 2:3])
            cx.fence(s, ins, "fs")
            ins = s.activation(lnst[:, 6:7], lnst[:, 2:3], AF.Square)
            cx.fence(s, ins, "fs")
            ins = s.mul(lnst[:, 6:7], lnst[:, 6:7], cst[:, 3:4])
            cx.fence(s, ins, "fs")
            ins = s.add(lnst[:, 3:4], lnst[:, 3:4], lnst[:, 6:7])
            cx.fence(s, ins, "fs")
            ins = s.activation(lnst[:, 7:8], lnst[:, 3:4], AF.Ln, bias=cst[:, 0:1])
            cx.fence(s, ins, "fs")
            ins = s.activation(lnst[:, 4:5], lnst[:, 7:8], AF.Exp, scale=cst[:, 1:2])
            cx.fence(s, ins, "fs")
            ins = s.mul(lnst[:, 5:6], lnst[:, 2:3], lnst[:, 4:5])
            cx.fence(s, ins, "fs")
            ins = s.mul(lnst[:, 5:6], lnst[:, 5:6], cst[:, 3:4])
            cx.fence(s, ins, "fs")
            ins = s.activation(xn_out, x_in, AF.Identity,
                               bias=lnst[:, 5:6], scale=lnst[:, 4:5])
            cx.inc(ins, tag, 1)

        ring = {"n": 0}
        unit_free = []

        def units_for(e):
            units = []
            if e < E:
                for i in range(0, NDE, 2):
                    n = 2 if i + 1 < NDE else 1
                    for nm, kind in (("gate", "g"), ("up", "u")):
                        off = MW_OFF[nm] + (e * NDE + i) * 128 * KD * 128
                        units.append((off, KD * 128, n, KD * 128,
                                      [(kind, i + k, k * KD * 128) for k in range(n)]))
                for i in range(0, 16, 2):
                    off = MW_OFF["down"] + (e * 16 + i) * 128 * NDE * 128
                    units.append((off, NDE * 128, 2, NDE * 128,
                                  [("d", i + k, k * NDE * 128) for k in range(2)]))
            else:
                for half in range(2):
                    for i in range(0, 11, 2):
                        n = 2 if i + 1 < 11 else 1
                        mi0 = half * 11 + i
                        for nm, kind in (("sg", "g"), ("su", "u")):
                            off = MW_OFF[nm] + mi0 * 128 * KD * 128
                            units.append((off, KD * 128, n, KD * 128,
                                          [(kind, half * 11 + i + k, k * KD * 128)
                                           for k in range(n)]))
                    for i in range(16):
                        off = (MW_OFF["sd"] + i * 128 * NDS * 128 + half * NDE * 128)
                        units.append((off, NDE * 128, 1, NDS * 128,
                                      [("d", half * 16 + i, 0)]))
            return units

        def issue_units(sy, units, events):
            for (off, blen, nblk, pstride, _blks), ev in zip(units, events):
                idx = ring["n"]; ring["n"] += 1
                if idx >= NSL:
                    sem_nm, cnt = unit_free[idx - NSL]
                    cx.w(sy, sem_nm, cnt)
                unit_free.append(ev)
                sl = idx % NSL
                if nblk == 1:
                    srcp = _ap(mw, off, [[pstride, 128], [1, blen]])
                else:
                    srcp = _ap(mw, off, [[blen, 128], [128 * blen, nblk], [1, blen]])
                cx.inc(sy.dma_start(out=moe_w[:, sl * SLW:sl * SLW + nblk * blen],
                                    in_=srcp), "wmd", 16)

        def events_for(e, pg0, pu0, pd0):
            ev = []
            if e < E:
                for i in range(0, NDE, 2):
                    f = min(i + 2, NDE)
                    ev.append(("pg", pg0 + f))
                    ev.append(("pu", pu0 + f))
                for i in range(0, 16, 2):
                    ev.append(("pd", pd0 + i + 2))
            else:
                for half in range(2):
                    for i in range(0, 11, 2):
                        f = min(i + 2, 11)
                        ev.append(("pg", pg0 + half * 11 + f))
                        ev.append(("pu", pu0 + half * 11 + f))
                    for i in range(16):
                        ev.append(("pd", pd0 + half * 16 + i + 1))
            return ev

        def unit_map(units, start_idx):
            m = {}
            for k, (off, blen, nblk, pstride, blks) in enumerate(units):
                idx = start_idx + k
                for kind, mi, coff in blks:
                    m[(kind, mi)] = (idx, (idx % NSL) * SLW + coff)
            return m

        for c in range(NCH):
            bA = {k: cx.val(k) for k in
                  ("dx", "lns", "tp", "xe", "rt", "vp", "vse", "tvp", "vte",
                   "sop", "sfe", "csm", "cse", "ctp", "cte", "cdd", "cbp",
                   "cbe", "ox1")}

            with nc.Block() as blk:
                @blk.sync
                def _(sy):
                    cx.w(sy, "ox1", bA["ox1"])
                    for tsub in range(TS):
                        if tsub >= 2:
                            cx.w(sy, "lns", bA["lns"] + tsub - 1)
                        cx.inc(sy.dma_start(
                            out=x_tile[:, (tsub % 2) * D:(tsub % 2) * D + D],
                            in_=_ap(T["x"], (c * CH + tsub * 128) * D,
                                    [[D, 128], [1, D]])), "dx", 16)

                @blk.gpsimd
                def _(g):
                    cx.w(g, "cte", bA["cte"] + 1)
                    cx.w(g, "cdd", bA["cdd"])
                    cx.inc(g.dma_start(out=T["g_tmp"][0:8, :], in_=comb_T[:, :]),
                           "cdd", 16)
                    cx.w(g, "cdd", bA["cdd"] + 16)
                    cx.inc(g.dma_start(
                        out=comb_rows[0:1, 0:E * CH],
                        in_=_ap(T["g_tmp"], 0, [[E * CH, 1], [1, E * CH]])), "cdd", 16)

                @blk.scalar
                def _(s):
                    if c == 0:
                        cx.w(s, "ldm", cx.val("ldm"))
                        cx.w(s, "ldh", cx.val("ldh"))
                    for tsub in range(TS):
                        cx.w(s, "dx", bA["dx"] + 16 * (tsub + 1))
                        if tsub >= 2:
                            cx.w(s, "tp", bA["tp"] + (tsub - 1) * KD)
                        ln_scalar(s, x_tile[:, (tsub % 2) * D:(tsub % 2) * D + D],
                                  xn_tm[:, (tsub % 2) * D:(tsub % 2) * D + D],
                                  s_fm[:, 0:D], "lns")
                    for tsub in range(TS):
                        cx.w(s, "vp", bA["vp"] + tsub + 1)
                        if tsub >= 2:
                            cx.w(s, "tvp", bA["tvp"] + tsub - 1)
                        ins = s.activation(
                            vs_t[:, (tsub % 2) * 128:(tsub % 2) * 128 + 128],
                            p_a[tsub % 2][:, 0:128], AF.Copy)
                        cx.inc(ins, "vse", 1)
                        cx.w(s, "csm", bA["csm"] + tsub * 3 + 1)
                        ins = s.activation(cstat[:, 8:16], cstat[:, 0:8], AF.Exp,
                                           bias=cstat[:, 16:17])
                        cx.inc(ins, "cse", 1)
                        for mi in range(16):
                            n = tsub * 16 + mi
                            cx.w(s, "sop", bA["sop"] + n + 1)
                            ins = s.activation(
                                s_fm[:, mi * 512 + tsub * 128:
                                     mi * 512 + tsub * 128 + 128],
                                p_a[2 + (n % 2)][:, 0:128], AF.Identity,
                                bias=c_pt[:, mi:mi + 1])
                            cx.inc(ins, "sfe", 1)

                @blk.tensor
                def _(t):
                    if c == 0:
                        cx.w(t, "ldh", cx.val("ldh"))
                        cx.w(t, "ldm", cx.val("ldm"))
                        cx.w(t, "id_v", cx.val("id_v"))
                    for tsub in range(TS):
                        cx.w(t, "lns", bA["lns"] + tsub + 1)
                        for ki in range(KD):
                            n = tsub * KD + ki
                            if n >= 2:
                                cx.w(t, "xe", bA["xe"] + n - 1)
                            ins = t.transpose(
                                p_t[n % 2][:, :],
                                xn_tm[:, (tsub % 2) * D + ki * 128:
                                      (tsub % 2) * D + ki * 128 + 128], ident[:, :])
                            cx.inc(ins, "tp", 1)
                    for tsub in range(TS):
                        cx.w(t, "xe", bA["xe"] + (tsub + 1) * KD)
                        if tsub >= 2:
                            cx.w(t, "vse", bA["vse"] + tsub - 1)
                        for ki in range(KD):
                            ins = t.matmul(
                                p_a[tsub % 2][:, 0:128],
                                xn_fm[:, ki * 512 + tsub * 128:
                                      ki * 512 + tsub * 128 + 128],
                                wv_sb[:, ki * 128:ki * 128 + 128],
                                start=(ki == 0), stop=(ki == KD - 1))
                        cx.inc(ins, "vp", 1)
                        if tsub >= 1:
                            cx.w(t, "csm", bA["csm"] + (tsub - 1) * 3 + 1)
                        for ki in range(KD):
                            ins = t.matmul(p_r8[:, :],
                                           xn_fm[:, ki * 512 + tsub * 128:
                                                 ki * 512 + tsub * 128 + 128],
                                           router_w[:, ki * 8:ki * 8 + 8],
                                           start=(ki == 0), stop=(ki == KD - 1))
                        cx.inc(ins, "rt", 1)
                        cx.w(t, "vse", bA["vse"] + tsub + 1)
                        if tsub < 2:
                            cx.w(t, "xe", bA["xe"] + TS * KD)
                        else:
                            cx.w(t, "vte", bA["vte"] + tsub - 1)
                        ins = t.transpose(p_t[tsub % 2][:, :],
                                          vs_t[:, (tsub % 2) * 128:
                                               (tsub % 2) * 128 + 128], ident[:, :])
                        cx.inc(ins, "tvp", 1)
                        cx.w(t, "vte", bA["vte"] + tsub + 1)
                        for mi in range(16):
                            n = tsub * 16 + mi
                            if n >= 2:
                                cx.w(t, "sfe", bA["sfe"] + n - 1)
                            ins = t.matmul(
                                p_a[2 + (n % 2)][:, 0:128],
                                ws_sb[:, mi * 128:mi * 128 + 128],
                                vsT[:, (tsub % 2) * 128:(tsub % 2) * 128 + 128],
                                start=True, stop=True)
                            cx.inc(ins, "sop", 1)
                    cx.w(t, "csm", bA["csm"] + TS * 3)
                    for tsub in range(TS):
                        ins = t.transpose(p_c8[:, tsub * 128:tsub * 128 + 128],
                                          comb_tm[:, tsub * 8:tsub * 8 + 8],
                                          identf[:, :])
                        cx.inc(ins, "ctp", 1)
                    cx.w(t, "cdd", bA["cdd"] + 32)
                    cx.w(t, "vse", bA["vse"] + TS)
                    for e in range(E):
                        if e >= 2:
                            cx.w(t, "cbe", bA["cbe"] + e - 1)
                        ins = t.matmul(p_a[e % 2][:, :], ones_r[0:1, 0:128],
                                       comb_rows[0:1, e * 512:e * 512 + 512],
                                       start=True, stop=True)
                        cx.inc(ins, "cbp", 1)

                @blk.vector
                def _(v):
                    for tsub in range(TS):
                        for ki in range(KD):
                            n = tsub * KD + ki
                            cx.w(v, "tp", bA["tp"] + n + 1)
                            ins = v.tensor_copy(
                                xn_fm[:, ki * 512 + tsub * 128:
                                      ki * 512 + tsub * 128 + 128],
                                p_t[n % 2][:, :])
                            cx.inc(ins, "xe", 1)
                    for tsub in range(TS):
                        cx.w(v, "rt", bA["rt"] + tsub + 1)
                        ins = v.tensor_copy(cstat[:, 0:8], p_r8[:, :])
                        cx.fence(v, ins, "fv")
                        ins = v.reduce_max(cstat[:, 16:17], cstat[:, 0:8], AX.X,
                                           negate=True)
                        cx.inc(ins, "csm", 1)
                        cx.w(v, "cse", bA["cse"] + tsub + 1)
                        ins = v.reduce_sum(cstat[:, 17:18], cstat[:, 8:16], AX.X)
                        cx.fence(v, ins, "fv")
                        ins = v.reciprocal(cstat[:, 18:19], cstat[:, 17:18])
                        cx.fence(v, ins, "fv")
                        ins = v.tensor_scalar(cstat[:, 0:8], cstat[:, 8:16],
                                              cstat[:, 18:19], None, ALU.mult)
                        cx.fence(v, ins, "fv")
                        ins = v.reduce_max(cstat[:, 19:20], cstat[:, 0:8], AX.X)
                        cx.fence(v, ins, "fv")
                        ins = v.tensor_scalar(cstat[:, 8:16], cstat[:, 0:8],
                                              cstat[:, 19:20], None, ALU.is_ge)
                        cx.fence(v, ins, "fv")
                        ins = v.tensor_sub(cstat[:, 8:16], cstat[:, 0:8],
                                           cstat[:, 8:16])
                        cx.fence(v, ins, "fv")
                        ins = v.reduce_max(cstat[:, 20:21], cstat[:, 8:16], AX.X)
                        cx.fence(v, ins, "fv")
                        ins = v.tensor_scalar(cstat[:, 8:16], cstat[:, 0:8],
                                              cstat[:, 20:21], None, ALU.is_ge)
                        cx.fence(v, ins, "fv")
                        ins = v.tensor_mul(comb_tm[:, tsub * 8:tsub * 8 + 8],
                                           cstat[:, 0:8], cstat[:, 8:16])
                        cx.inc(ins, "csm", 2)
                        cx.w(v, "tvp", bA["tvp"] + tsub + 1)
                        if tsub >= 2:
                            cx.w(v, "sop", bA["sop"] + (tsub - 1) * 16)
                        ins = v.tensor_copy(vsT[:, (tsub % 2) * 128:
                                                (tsub % 2) * 128 + 128],
                                            p_t[tsub % 2][:, :])
                        cx.inc(ins, "vte", 1)
                    cx.w(v, "ctp", bA["ctp"] + TS)
                    ins = v.tensor_copy(comb_T[:, :], p_c8[0:8, 0:512])
                    cx.inc(ins, "cte", 1)
                    for e in range(E):
                        cx.w(v, "cbp", bA["cbp"] + e + 1)
                        ins = v.tensor_copy(comb_bc[:, e * 512:e * 512 + 512],
                                            p_a[e % 2][:, :])
                        cx.inc(ins, "cbe", 1)

            for e in range(E + 1):
                _p1_stage_B(nc, cx, c, e, xn_fm, comb_bc, h_buf, racc, silu_t,
                            h_t, p_a, moe_w, NSL, SLW, units_for, unit_map,
                            issue_units, events_for, ring)

            _p1_stage_CE(nc, cx, T, c, ident, racc, g_fm, s_fm, x_tile, xn_tm,
                         sg_tm, xn2f_t, h_buf, lnst, p_t, ln_scalar)

        with nc.Block() as blk:
            @blk.sync
            def _(sy):
                cx.wall(sy, "ox1")
                cx.wall(sy, "oxn2")
        if own_cx:
            _clear_sems(nc, cx)


def _p1_stage_B(nc, cx, c, e, xn_fm, comb_bc, h_buf, racc, silu_t, h_t, p_a,
                moe_w, NSL, SLW, units_for, unit_map, issue_units, events_for,
                ring):
    units = units_for(e)
    m = unit_map(units, ring["n"])
    shared = (e == E)
    ng = 11 if shared else NDE
    halves = 2 if shared else 1

    bB = {k: cx.val(k) for k in ("pg", "pu", "psil", "hv1", "hv", "pd", "rv",
                                 "wmd")}

    with nc.Block() as blk:
        @blk.sync
        def _(sy):
            issue_units(sy, units, events_for(e, bB["pg"], bB["pu"], bB["pd"]))

        @blk.tensor
        def _(t):
            for half in range(halves):
                for i in range(ng):
                    key = half * 11 + i
                    uidx_g, coff_g = m[("g", key)]
                    uidx_u, coff_u = m[("u", key)]
                    cx.w(t, "wmd", 16 * (max(uidx_g, uidx_u) + 1))
                    nh = half * ng + i
                    if nh >= 2:
                        cx.w(t, "hv", bB["hv"] + nh - 1)
                    sg_ = 2 * (nh % 2)
                    for ki in range(KD):
                        ins = t.matmul(
                            p_a[sg_][:, :],
                            moe_w[:, coff_g + ki * 128:coff_g + ki * 128 + 128],
                            xn_fm[:, ki * 512:ki * 512 + 512],
                            start=(ki == 0), stop=(ki == KD - 1))
                    cx.inc(ins, "pg", 1)
                    for ki in range(KD):
                        ins = t.matmul(
                            p_a[sg_ + 1][:, :],
                            moe_w[:, coff_u + ki * 128:coff_u + ki * 128 + 128],
                            xn_fm[:, ki * 512:ki * 512 + 512],
                            start=(ki == 0), stop=(ki == KD - 1))
                    cx.inc(ins, "pu", 1)
                for mi in range(16):
                    key = half * 16 + mi
                    uidx_d, coff_d = m[("d", key)]
                    cx.w(t, "wmd", 16 * (uidx_d + 1))
                    nd = half * 16 + mi
                    if mi == 0:
                        cx.w(t, "hv", bB["hv"] + (half + 1) * ng)
                    ndg = half * 16 + mi
                    if ndg >= 2:
                        cx.w(t, "rv", bB["rv"] + ndg - 1)
                    for ki in range(NDE):
                        ins = t.matmul(
                            p_a[ndg % 2][:, :],
                            moe_w[:, coff_d + ki * 128:coff_d + ki * 128 + 128],
                            h_buf[:, ki * 512:ki * 512 + 512],
                            start=(ki == 0), stop=(ki == NDE - 1))
                    cx.inc(ins, "pd", 1)

        @blk.scalar
        def _(s):
            for half in range(halves):
                for i in range(ng):
                    nh = half * ng + i
                    cx.w(s, "pg", bB["pg"] + nh + 1)
                    if nh >= 2:
                        cx.w(s, "hv" if shared else "hv1", bB["hv" if shared
                             else "hv1"] + nh - 1)
                    ins = s.activation(silu_t[:, (nh % 2) * CH:(nh % 2) * CH + CH],
                                       p_a[2 * (nh % 2)][:, :], AF.Silu)
                    cx.inc(ins, "psil", 1)

        @blk.gpsimd
        def _(g):
            if not shared:
                for i in range(ng):
                    cx.w(g, "hv1", bB["hv1"] + i + 1)
                    ins = g.tensor_mul(h_buf[:, i * 512:i * 512 + 512],
                                       h_t[:, (i % 2) * CH:(i % 2) * CH + CH],
                                       comb_bc[:, e * 512:e * 512 + 512])
                    cx.inc(ins, "hv", 1)

        @blk.vector
        def _(v):
            for half in range(halves):
                for i in range(ng):
                    nh = half * ng + i
                    cx.w(v, "psil", bB["psil"] + nh + 1)
                    cx.w(v, "pu", bB["pu"] + nh + 1)
                    sl = (nh % 2)
                    if shared:
                        ins = v.scalar_tensor_tensor(
                            h_buf[:, i * 512:i * 512 + 512],
                            p_a[2 * sl + 1][:, :], 1.0,
                            silu_t[:, sl * CH:sl * CH + CH], ALU.mult, ALU.mult)
                        cx.inc(ins, "hv", 1)
                    else:
                        if nh >= 2:
                            cx.w(v, "hv", bB["hv"] + nh - 1)
                        ins = v.scalar_tensor_tensor(
                            h_t[:, sl * CH:sl * CH + CH],
                            p_a[2 * sl + 1][:, :], 1.0,
                            silu_t[:, sl * CH:sl * CH + CH], ALU.mult, ALU.mult)
                        cx.inc(ins, "hv1", 1)
                for mi in range(16):
                    nd = half * 16 + mi
                    cx.w(v, "pd", bB["pd"] + nd + 1)
                    if e == 0:
                        ins = v.tensor_scalar(racc[:, mi * 512:mi * 512 + 512],
                                              p_a[nd % 2][:, :], 1.0, None, ALU.mult)
                    else:
                        ins = v.scalar_tensor_tensor(
                            racc[:, mi * 512:mi * 512 + 512],
                            p_a[nd % 2][:, :], 1.0,
                            racc[:, mi * 512:mi * 512 + 512], ALU.mult, ALU.add)
                    cx.inc(ins, "rv", 1)


def _p1_stage_CE(nc, cx, T, c, ident, racc, g_fm, s_fm, x_tile, xn_tm, sg_tm,
                 xn2f_t, ep_junk, lnst, p_t, ln_scalar):
    bC = {k: cx.val(k) for k in ("gs", "sgv", "etp", "ete", "dx2", "x1v",
                                 "lns2", "ox1", "oxn2")}

    with nc.Block() as blk:
        @blk.scalar
        def _(s):
            for mi in range(16):
                ins = s.activation(g_fm[:, mi * 512:mi * 512 + 512],
                                   racc[:, mi * 512:mi * 512 + 512], AF.Silu)
                cx.inc(ins, "gs", 1)
            for tsub in range(TS):
                cx.w(s, "x1v", bC["x1v"] + tsub + 1)
                if tsub >= 2:
                    cx.w(s, "etp", bC["etp"] + tsub * 32 - 16)
                ln_scalar(s, x_tile[:, (tsub % 2) * D:(tsub % 2) * D + D],
                          xn_tm[:, (tsub % 2) * D:(tsub % 2) * D + D],
                          ep_junk[:, 0:D], "lns2")

        @blk.vector
        def _(v):
            for mi in range(16):
                cx.w(v, "gs", bC["gs"] + mi + 1)
                ins = v.tensor_mul(s_fm[:, mi * 512:mi * 512 + 512],
                                   s_fm[:, mi * 512:mi * 512 + 512],
                                   g_fm[:, mi * 512:mi * 512 + 512])
                cx.inc(ins, "sgv", 1)
            for tsub in range(TS):
                for mi in range(16):
                    n = tsub * 32 + mi
                    cx.w(v, "etp", bC["etp"] + n + 1)
                    ins = v.tensor_copy(sg_tm[:, mi * 128:mi * 128 + 128],
                                        p_t[n % 2][:, :])
                    cx.inc(ins, "ete", 1)
                cx.wall(v, "ete")
                cx.w(v, "dx2", bC["dx2"] + 16 * (tsub + 1))
                if tsub >= 2:
                    cx.w(v, "ox1", bC["ox1"] + 16 * (tsub - 1))
                ins = v.tensor_add(x_tile[:, (tsub % 2) * D:(tsub % 2) * D + D],
                                   x_tile[:, (tsub % 2) * D:(tsub % 2) * D + D],
                                   sg_tm[:, :])
                cx.inc(ins, "x1v", 1)
                if tsub >= 1:
                    cx.w(v, "oxn2", bC["oxn2"] + 16 * tsub)
                for mi in range(16):
                    n = tsub * 32 + 16 + mi
                    cx.w(v, "etp", bC["etp"] + n + 1)
                    ins = v.tensor_copy(xn2f_t[:, mi * 128:mi * 128 + 128],
                                        p_t[n % 2][:, :])
                    cx.inc(ins, "ete", 1)

        @blk.tensor
        def _(t):
            for tsub in range(TS):
                for mi in range(16):
                    n = tsub * 32 + mi
                    cx.w(t, "sgv", bC["sgv"] + mi + 1)
                    if n >= 2:
                        cx.w(t, "ete", bC["ete"] + n - 1)
                    ins = t.transpose(
                        p_t[n % 2][:, :],
                        s_fm[:, mi * 512 + tsub * 128:mi * 512 + tsub * 128 + 128],
                        ident[:, :])
                    cx.inc(ins, "etp", 1)
                cx.w(t, "lns2", bC["lns2"] + tsub + 1)
                for mi in range(16):
                    n = tsub * 32 + 16 + mi
                    if n >= 2:
                        cx.w(t, "ete", bC["ete"] + n - 1)
                    ins = t.transpose(
                        p_t[n % 2][:, :],
                        xn_tm[:, (tsub % 2) * D + mi * 128:
                              (tsub % 2) * D + mi * 128 + 128], ident[:, :])
                    cx.inc(ins, "etp", 1)

        @blk.sync
        def _(sy):
            for tsub in range(TS + 2):
                if tsub >= 2:
                    ts2 = tsub - 2
                    cx.w(sy, "x1v", bC["x1v"] + ts2 + 1)
                    cx.inc(sy.dma_start(
                        out=_ap(T["g_x1"], (c * CH + ts2 * 128) * D,
                                [[D, 128], [1, D]]),
                        in_=x_tile[:, (ts2 % 2) * D:(ts2 % 2) * D + D]), "ox1", 16)
                    cx.w(sy, "ete", bC["ete"] + ts2 * 32 + 32)
                    cx.inc(sy.dma_start(
                        out=_ap(T["g_xn2"], c * CH + ts2 * 128,
                                [[S, 128], [128 * S, KD], [1, 128]]),
                        in_=xn2f_t[:, :].rearrange("p (k m) -> p k m", m=128)),
                        "oxn2", 16)
                if tsub < TS:
                    if tsub >= 2:
                        cx.w(sy, "lns2", bC["lns2"] + tsub - 1)
                        cx.w(sy, "ox1", bC["ox1"] + 16 * (tsub - 1))
                    cx.inc(sy.dma_start(
                        out=x_tile[:, (tsub % 2) * D:(tsub % 2) * D + D],
                        in_=_ap(T["x"], (c * CH + tsub * 128) * D,
                                [[D, 128], [1, D]])), "dx2", 16)


def emit_p2(nc, T, cx=None):
    with ExitStack() as st:
        own_cx = cx is None
        cx = cx if cx is not None else Ctx(nc, st)
        sb = lambda nm, shp, dt: st.enter_context(nc.sbuf_tensor(nm, shp, dt))

        ident = sb("p2_id", [128, 128], BF16)
        iota_s = sb("p2_io", [128, 128], F32)
        ones_r = sb("p2_on", [1, 512], BF16)
        xn2s = sb("p2_xn", [128, KD * S], BF16)
        wq_w = sb("p2_w", [128, 2 * KD * 128], BF16)
        qk_sb = sb("p2_qk", [128, 2 * S], BF16)
        v_sb = sb("p2_v", [128, 2 * 32 * 128], BF16)
        b1_sb = sb("p2_b1", [48, 128], BF16)
        b1_pt = sb("p2_b1p", [128, 48], BF16)
        bv_row = sb("p2_bv", [1, D], BF16)
        cos_sb = sb("p2_cos", [32, S], BF16)
        sin_sb = sb("p2_sin", [32, S], BF16)
        pmat_sb = sb("p2_pm", [32, 32], BF16)
        rt1 = sb("p2_rt1", [32, 512], F32)
        rt2 = sb("p2_rt2", [32, 512], F32)

        p_f = [st.enter_context(nc.psum_tensor(f"p2_pf{i}", [128, 512], F32))
               for i in range(2)]
        p_v = [st.enter_context(nc.psum_tensor(f"p2_pv{i}", [128, 128], F32))
               for i in range(2)]
        p_r = [st.enter_context(nc.psum_tensor(f"p2_pr{i}", [32, 512], F32))
               for i in range(2)]
        p_bt = st.enter_context(nc.psum_tensor("p2_pbt", [128, 48], BF16))

        _emit_ident(nc, cx, ident, iota_s)

        with nc.Block() as blk:
            @blk.sync
            def _(sy):
                for ki in range(KD):
                    cx.inc(sy.dma_start(
                        out=xn2s[:, ki * S:ki * S + S],
                        in_=_ap(T["g_xn2"], ki * 128 * S, [[S, 128], [1, S]])),
                        "xnl", 16)


            @blk.gpsimd
            def _(g):
                g.memset(ones_r[:, :], 1.0)
                cx.inc(g.dma_start(out=cos_sb[:, :], in_=T["cosb"][:, :]), "cl", 16)
                cx.inc(g.dma_start(out=sin_sb[:, :], in_=T["sinb"][:, :]), "cl", 16)
                cx.inc(g.dma_start(out=b1_sb[:, :],
                                   in_=_ap(T["b1"], 0, [[128, 48], [1, 128]])),
                       "cl", 16)
                cx.inc(g.dma_start(out=bv_row[0:1, :],
                                   in_=_ap(T["b1"], 256, [[1, 1], [384, 16], [1, 128]])),
                       "cl", 16)
                cx.inc(g.dma_start(out=pmat_sb[:, :], in_=T["pmat"][:, :]), "cl", 16)

        with nc.Block() as blk:
            @blk.tensor
            def _(t):
                cx.w(t, "cl", cx.val("cl"))
                cx.w(t, "id_v", cx.val("id_v"))
                ins = t.transpose(p_bt[:, 0:48], b1_sb[:, :], ident[0:48, 0:48])
                cx.inc(ins, "bt", 1)

            @blk.vector
            def _(v):
                cx.w(v, "bt", 1)
                cx.inc(v.tensor_copy(b1_pt[:, :], p_bt[:, 0:48]), "btv", 1)

        wfree = []
        nqk = [0]
        nv = [0]

        with nc.Block() as blk:
            @blk.gpsimd
            def _(g):
                for mi in range(48):
                    part = mi % 3
                    if mi >= 2:
                        cx.w(g, *wfree[mi - 2])
                    cx.inc(g.dma_start(
                        out=wq_w[:, (mi % 2) * 2048:(mi % 2) * 2048 + 2048],
                        in_=_ap(T["wqkv1"], mi * 128,
                                [[3 * D, 128], [128 * 3 * D, KD], [1, 128]])),
                        "wql", 16)
                    if part < 2:
                        k = len([m for m in range(mi + 1) if m % 3 < 2])
                        wfree.append(("qe2", 8 * k))
                    else:
                        k = len([m for m in range(mi + 1) if m % 3 == 2])
                        wfree.append(("ve2", 32 * k))

            @blk.tensor
            def _(t):
                cx.w(t, "xnl", 16 * KD)
                cx.w(t, "cl", cx.val("cl"))
                cx.w(t, "btv", 1)
                gq = 0
                gv = 0
                gr = 0
                for mi in range(48):
                    part, h = mi % 3, mi // 3
                    sl = (mi % 2) * 2048
                    cx.w(t, "wql", 16 * (mi + 1))
                    if part < 2:
                        for tc in range(8):
                            if gq >= 2:
                                cx.w(t, "qe2", gq - 1)
                            for ki in range(KD):
                                ins = t.matmul(
                                    p_f[gq % 2][:, :],
                                    wq_w[:, sl + ki * 128:sl + ki * 128 + 128],
                                    xn2s[:, ki * S + tc * 512:ki * S + tc * 512 + 512],
                                    start=(ki == 0), stop=(ki == KD - 1))
                            cx.inc(ins, "qp2", 1)
                            gq += 1
                        for tc in range(8):
                            cx.w(t, "qe2", gq - 8 + tc + 1)
                            if gr >= 2:
                                cx.w(t, "rv2", gr - 1)
                            qsl = ((gq - 8) // 8 % 2) * S
                            ins = t.matmul(
                                p_r[gr % 2][:, :],
                                pmat_sb[:, :],
                                qk_sb[0:32, qsl + tc * 512:qsl + tc * 512 + 512],
                                start=True, stop=True)
                            cx.inc(ins, "rp2", 1)
                            gr += 1
                    else:
                        for tsub in range(32):
                            if gv >= 2:
                                cx.w(t, "ve2", gv - 1)
                            for ki in range(KD):
                                t.matmul(
                                    p_v[gv % 2][:, :],
                                    xn2s[:, ki * S + tsub * 128:ki * S + tsub * 128 + 128],
                                    wq_w[:, sl + ki * 128:sl + ki * 128 + 128],
                                    start=(ki == 0), stop=False)
                            ins = t.matmul(p_v[gv % 2][:, :], ones_r[0:1, 0:128],
                                           bv_row[0:1, h * 128:h * 128 + 128],
                                           start=False, stop=True)
                            cx.inc(ins, "vp2", 1)
                            gv += 1

            @blk.scalar
            def _(s):
                gq = 0
                for mi in range(48):
                    part, h = mi % 3, mi // 3
                    if part == 2:
                        continue
                    nq = gq // 8
                    qsl = (nq % 2) * S
                    for tc in range(8):
                        cx.w(s, "qp2", gq + 1)
                        if nq >= 2 and tc == 0:
                            cx.w(s, "oqk", 16 * (nq - 1))
                        ins = s.activation(qk_sb[:, qsl + tc * 512:qsl + tc * 512 + 512],
                                           p_f[gq % 2][:, :], AF.Identity,
                                           bias=b1_pt[:, mi:mi + 1])
                        cx.inc(ins, "qe2", 1)
                        gq += 1

            @blk.vector
            def _(v):
                gq = gr = gv = 0
                for mi in range(48):
                    part, h = mi % 3, mi // 3
                    if part < 2:
                        nq = gq // 8
                        qsl = (nq % 2) * S
                        gq += 8
                        for tc in range(8):
                            cx.w(v, "rp2", gr + 1)
                            ins = v.tensor_mul(rt1[:, :], p_r[gr % 2][:, :],
                                               sin_sb[:, tc * 512:tc * 512 + 512])
                            ins = v.tensor_mul(
                                rt2[:, :],
                                qk_sb[0:32, qsl + tc * 512:qsl + tc * 512 + 512],
                                cos_sb[:, tc * 512:tc * 512 + 512])
                            cx.fence(v, ins, "fv")
                            ins = v.tensor_add(
                                qk_sb[0:32, qsl + tc * 512:qsl + tc * 512 + 512],
                                rt1[:, :], rt2[:, :])
                            cx.inc(ins, "rv2", 1)
                            gr += 1
                    else:
                        nvv = gv // 32
                        vsl = (nvv % 2) * S
                        for tsub in range(32):
                            cx.w(v, "vp2", gv + 1)
                            if nvv >= 2 and tsub == 0:
                                cx.w(v, "ovt", 16 * (nvv - 1))
                            ins = v.tensor_copy(
                                v_sb[:, vsl + tsub * 128:vsl + tsub * 128 + 128],
                                p_v[gv % 2][:, :])
                            cx.inc(ins, "ve2", 1)
                            gv += 1

            @blk.sync
            def _(sy):
                gq = gv = 0
                for mi in range(48):
                    part, h = mi % 3, mi // 3
                    if part < 2:
                        nq = gq // 8
                        qsl = (nq % 2) * S
                        gq += 8
                        cx.w(sy, "rv2", 8 * (nq + 1))
                        cx.inc(sy.dma_start(
                            out=_ap(T["g_qk"], (part * NH + h) * 128 * S,
                                    [[S, 128], [1, S]]),
                            in_=qk_sb[:, qsl:qsl + S]), "oqk", 16)
                    else:
                        nvv = gv // 32
                        vsl = (nvv % 2) * S
                        gv += 32
                        cx.w(sy, "ve2", 32 * (nvv + 1))
                        cx.inc(sy.dma_start(
                            out=_ap(T["g_vtm"], h * 128,
                                    [[D, 128], [128 * D, 32], [1, 128]]),
                            in_=v_sb[:, vsl:vsl + S].rearrange(
                                "p (k m) -> p k m", m=128)), "ovt", 16)

        with nc.Block() as blk:
            @blk.sync
            def _(sy):
                cx.wall(sy, "oqk")
                cx.wall(sy, "ovt")
        if own_cx:
            _clear_sems(nc, cx)


def emit_p3(nc, T, cx=None):
    with ExitStack() as st:
        own_cx = cx is None
        cx = cx if cx is not None else Ctx(nc, st)
        sb = lambda nm, shp, dt: st.enter_context(nc.sbuf_tensor(nm, shp, dt))

        ident = sb("p3_id", [128, 128], BF16)
        iota_s = sb("p3_io", [128, 128], F32)
        ones_c = sb("p3_onc", [128, 1], BF16)
        ones_rb = sb("p3_onb", [1, 128], BF16)
        attn_sb = sb("p3_at", [128, NH * S], BF16)
        mask_sb = sb("p3_mk", [128, 4 * 512], F32)
        bo_sb = sb("p3_bo", [16, 128], BF16)
        bo_pt = sb("p3_bop", [128, 16], BF16)

        _emit_ident(nc, cx, ident, iota_s)

        with nc.Block() as blk:
            @blk.sync
            def _(sy):
                cx.inc(sy.dma_start(out=mask_sb[:, :], in_=T["mask4"][:, :]), "cl3", 16)

            @blk.gpsimd
            def _(g):
                g.memset(ones_c[:, :], 1.0)
                g.memset(ones_rb[:, :], 1.0)
                cx.inc(g.dma_start(out=bo_sb[:, :],
                                   in_=_ap(T["bo1"], 0, [[128, 16], [1, 128]])),
                       "cl3g", 16)

        with ExitStack() as st2:
            sb2 = lambda nm, shp, dt: st2.enter_context(nc.sbuf_tensor(nm, shp, dt))
            q_sb = sb2("p3_q", [128, 2 * S], BF16)
            k_sb = sb2("p3_k", [128, 2 * S], BF16)
            v_sb = sb2("p3_v", [128, 2 * S], BF16)
            expT = sb2("p3_e", [128, 2 * 512], BF16)
            msk_s = sb2("p3_ms", [128, 2 * 512], F32)
            rc_sb = sb2("p3_rc", [1, 2 * 512], BF16)
            rcf32 = sb2("p3_rcf", [1, 512], F32)
            rb_sb = sb2("p3_rb", [128, 2 * 512], BF16)
            p_s = [st2.enter_context(nc.psum_tensor(f"p3_ps{i}", [128, 512], F32))
                   for i in range(2)]
            p_o = [st2.enter_context(nc.psum_tensor(f"p3_po{i}", [128, 512], F32))
                   for i in range(2)]
            p_d = [st2.enter_context(nc.psum_tensor(f"p3_pd{i}", [1, 512], F32))
                   for i in range(2)]
            p_b = st2.enter_context(nc.psum_tensor("p3_pb", [128, 512], F32))

            def kts(qc):
                return 4 * qc + 4

            with nc.Block() as blk:
                @blk.sync
                def _(sy):
                    for h in range(NH):
                        if h >= 2:
                            cx.w(sy, "aat", 8 * (h - 1))
                        hsl = (h % 2) * S
                        cx.inc(sy.dma_start(
                            out=q_sb[:, hsl:hsl + S],
                            in_=_ap(T["g_qk"], h * 128 * S, [[S, 128], [1, S]])),
                            "qkl", 16)
                        cx.inc(sy.dma_start(
                            out=k_sb[:, hsl:hsl + S],
                            in_=_ap(T["g_qk"], (NH + h) * 128 * S,
                                    [[S, 128], [1, S]])), "qkl", 16)
                        cx.inc(sy.dma_start(
                            out=v_sb[:, hsl:hsl + S],
                            in_=_ap(T["g_vtm"], h * 128,
                                    [[D, 128], [128 * D, 32], [1, 128]])),
                            "qkl", 16)

                @blk.tensor
                def _(t):
                    cx.w(t, "cl3", 16)
                    npst = 0
                    for h in range(NH):
                        hsl = (h % 2) * S
                        cx.w(t, "qkl", 48 * (h + 1))
                        for qc in range(8):
                            aq = h * 8 + qc
                            nkt = kts(qc)
                            base_pst = npst
                            for kt in range(nkt):
                                if npst >= 2:
                                    cx.w(t, "xps", npst - 1)
                                ins = t.matmul(
                                    p_s[kt % 2][:, :],
                                    k_sb[:, hsl + kt * 128:hsl + kt * 128 + 128],
                                    q_sb[:, hsl + qc * 512:hsl + qc * 512 + 512],
                                    start=True, stop=True)
                                cx.inc(ins, "pst", 1)
                                npst += 1
                                ktp = kt - 1
                                if ktp >= 0:
                                    _p3_pv_den(cx, t, h, qc, ktp, nkt, aq, hsl,
                                               base_pst, v_sb, expT, p_o, p_d,
                                               ones_c)
                            _p3_pv_den(cx, t, h, qc, nkt - 1, nkt, aq, hsl,
                                       base_pst, v_sb, expT, p_o, p_d, ones_c)
                            cx.w(t, "rcv", aq + 1)
                            if aq >= 1:
                                cx.w(t, "rbe", aq)
                            ins = t.matmul(p_b[:, :], ones_rb[0:1, :],
                                           rc_sb[0:1, (aq % 2) * 512:(aq % 2) * 512 + 512],
                                           start=True, stop=True)
                            cx.inc(ins, "bcp", 1)

                @blk.scalar
                def _(s):
                    npst = 0
                    for h in range(NH):
                        for qc in range(8):
                            aq = h * 8 + qc
                            nkt = kts(qc)
                            for kt in range(nkt):
                                cx.w(s, "pst", npst + 1)
                                if npst >= 2:
                                    cx.w(s, "pvp", npst - 1)
                                if kt >= 4 * qc:
                                    cx.w(s, "mskv", cx.val("mskv") - 0) \
                                        if False else None
                                    cx.w(s, "mskv", _p3_mskcnt(h, qc, kt) + 1)
                                    src_ap = msk_s[:, (kt % 2) * 512:(kt % 2) * 512 + 512]
                                else:
                                    src_ap = p_s[kt % 2][:, :]
                                ins = s.activation(
                                    expT[:, (kt % 2) * 512:(kt % 2) * 512 + 512],
                                    src_ap, AF.Exp)
                                cx.inc(ins, "xps", 1)
                                npst += 1
                            cx.w(s, "bcp", aq + 1)
                            if aq >= 1:
                                cx.w(s, "aat", aq)
                            ins = s.activation(
                                rb_sb[:, (aq % 2) * 512:(aq % 2) * 512 + 512],
                                p_b[:, :], AF.Copy)
                            cx.inc(ins, "rbe", 1)

                @blk.vector
                def _(v):
                    npst = 0
                    for h in range(NH):
                        for qc in range(8):
                            aq = h * 8 + qc
                            nkt = kts(qc)
                            for kt in range(nkt):
                                if kt >= 4 * qc:
                                    cx.w(v, "pst", npst + 1)
                                    if npst >= 2:
                                        cx.w(v, "xps", npst - 1)
                                    ins = v.tensor_add(
                                        msk_s[:, (kt % 2) * 512:(kt % 2) * 512 + 512],
                                        p_s[kt % 2][:, :],
                                        mask_sb[:, (kt - 4 * qc) * 512:
                                                (kt - 4 * qc) * 512 + 512])
                                    cx.inc(ins, "mskv", 1)
                                npst += 1
                            cx.w(v, "pdn", aq + 1)
                            ins = v.reciprocal(rcf32[0:1, :], p_d[aq % 2][0:1, :])
                            cx.fence(v, ins, "fv")
                            ins = v.tensor_copy(
                                rc_sb[0:1, (aq % 2) * 512:(aq % 2) * 512 + 512],
                                rcf32[0:1, :])
                            cx.inc(ins, "rcv", 1)
                            cx.w(v, "rbe", aq + 1)
                            ins = v.tensor_mul(
                                attn_sb[:, h * S + qc * 512:h * S + qc * 512 + 512],
                                p_o[aq % 2][:, :],
                                rb_sb[:, (aq % 2) * 512:(aq % 2) * 512 + 512])
                            cx.inc(ins, "aat", 1)

        with ExitStack() as st3:
            sb3 = lambda nm, shp, dt: st3.enter_context(nc.sbuf_tensor(nm, shp, dt))
            wo_w = sb3("p3_wo", [128, 2 * KD * 128], BF16)
            s2row = sb3("p3_s2", [128, 2 * S], BF16)
            p_f = [st3.enter_context(nc.psum_tensor(f"p3_pf{i}", [128, 512], F32))
                   for i in range(2)]
            p_bt = st3.enter_context(nc.psum_tensor("p3_pbt", [128, 16], BF16))

            with nc.Block() as blk:
                @blk.tensor
                def _(t):
                    cx.w(t, "cl3g", cx.val("cl3g"))
                    cx.w(t, "id_v", cx.val("id_v"))
                    ins = t.transpose(p_bt[:, 0:16], bo_sb[:, :], ident[0:16, 0:16])
                    cx.inc(ins, "bt3", 1)

                @blk.vector
                def _(v):
                    cx.w(v, "bt3", 1)
                    cx.inc(v.tensor_copy(bo_pt[:, :], p_bt[:, 0:16]), "btv3", 1)

            with nc.Block() as blk:
                @blk.gpsimd
                def _(g):
                    for mi in range(16):
                        if mi >= 2:
                            cx.w(g, "s2e", 8 * (mi - 1))
                        cx.inc(g.dma_start(
                            out=wo_w[:, (mi % 2) * 2048:(mi % 2) * 2048 + 2048],
                            in_=_ap(T["wout1"], mi * 128,
                                    [[D, 128], [128 * D, KD], [1, 128]])),
                            "wol", 16)

                @blk.tensor
                def _(t):
                    cx.w(t, "btv3", 1)
                    for mi in range(16):
                        sl = (mi % 2) * 2048
                        cx.w(t, "wol", 16 * (mi + 1))
                        for tc in range(8):
                            g2 = mi * 8 + tc
                            if g2 >= 2:
                                cx.w(t, "s2e", g2 - 1)
                            for ki in range(KD):
                                ins = t.matmul(
                                    p_f[g2 % 2][:, :],
                                    wo_w[:, sl + ki * 128:sl + ki * 128 + 128],
                                    attn_sb[:, ki * S + tc * 512:ki * S + tc * 512 + 512],
                                    start=(ki == 0), stop=(ki == KD - 1))
                            cx.inc(ins, "s2p", 1)

                @blk.scalar
                def _(s):
                    for mi in range(16):
                        ssl = (mi % 2) * S
                        for tc in range(8):
                            g2 = mi * 8 + tc
                            cx.w(s, "s2p", g2 + 1)
                            if mi >= 2 and tc == 0:
                                cx.w(s, "os2", 16 * (mi - 1))
                            ins = s.activation(
                                s2row[:, ssl + tc * 512:ssl + tc * 512 + 512],
                                p_f[g2 % 2][:, :], AF.Identity,
                                bias=bo_pt[:, mi:mi + 1])
                            cx.inc(ins, "s2e", 1)

                @blk.sync
                def _(sy):
                    for mi in range(16):
                        ssl = (mi % 2) * S
                        cx.w(sy, "s2e", 8 * (mi + 1))
                        cx.inc(sy.dma_start(
                            out=_ap(T["g_s2"], mi * 128 * S, [[S, 128], [1, S]]),
                            in_=s2row[:, ssl:ssl + S]), "os2", 16)

            with nc.Block() as blk:
                @blk.sync
                def _(sy):
                    cx.wall(sy, "os2")
        if own_cx:
            _clear_sems(nc, cx)


def _p3_mskcnt(h, qc, kt):
    n = h * 32
    for q2 in range(qc):
        n += 4
    n += kt - 4 * qc
    return n


def _p3_pv_den(cx, t, h, qc, kt, nkt, aq, hsl, base_pst, v_sb, expT, p_o, p_d,
               ones_c):
    cx.w(t, "xps", base_pst + kt + 1)
    if kt == 0 and aq >= 2:
        cx.w(t, "aat", aq - 1)
    ins = t.matmul(p_o[aq % 2][:, :],
                   v_sb[:, hsl + kt * 128:hsl + kt * 128 + 128],
                   expT[:, (kt % 2) * 512:(kt % 2) * 512 + 512],
                   start=(kt == 0), stop=(kt == nkt - 1))
    cx.inc(ins, "pvp", 1)
    if kt == 0 and aq >= 2:
        cx.w(t, "rcv", aq - 1)
    ins = t.matmul(p_d[aq % 2][0:1, :], ones_c[:, 0:1],
                   expT[:, (kt % 2) * 512:(kt % 2) * 512 + 512],
                   start=(kt == 0), stop=(kt == nkt - 1))
    if kt == nkt - 1:
        cx.inc(ins, "pdn", 1)


def emit_p4(nc, T, cx=None):
    with ExitStack() as st:
        own_cx = cx is None
        cx = cx if cx is not None else Ctx(nc, st)
        sb = lambda nm, shp, dt: st.enter_context(nc.sbuf_tensor(nm, shp, dt))

        ident = sb("p4_id", [128, 128], BF16)
        iota_s = sb("p4_io", [128, 128], F32)
        ones_rb = sb("p4_onb", [1, 128], BF16)
        scl_sb = sb("p4_scl", [1, 4], F32)
        scl_hl = sb("p4_shl", [1, 4], BF16)
        scl_bc = sb("p4_sbc", [128, 1], F32)
        xn2c = sb("p4_xn", [128, KD * 512], BF16)
        s2c = sb("p4_s2", [128, KD * 512], BF16)
        h2 = sb("p4_h2", [128, NHF * 512], BF16)
        wfc_w = sb("p4_wf", [128, 2 * KD * 128], BF16)
        wpj_w = sb("p4_wp", [128, 2 * NHF * 128], BF16)
        rl_sb = sb("p4_rl", [128, 2 * 512], BF16)
        g2_sb = sb("p4_g2", [128, 2 * 512], BF16)
        sgt = sb("p4_sgt", [128, TS * D], BF16)
        x1t = sb("p4_x1", [128, 2 * D], F32)

        p_a = [st.enter_context(nc.psum_tensor(f"p4_pa{i}", [128, 512], F32))
               for i in range(2)]
        p_t = [st.enter_context(nc.psum_tensor(f"p4_pt{i}", [128, 128], BF16))
               for i in range(2)]
        p_b = st.enter_context(nc.psum_tensor("p4_pb", [128, 1], F32))

        _emit_ident(nc, cx, ident, iota_s)

        with nc.Block() as blk:
            @blk.gpsimd
            def _(g):
                g.memset(ones_rb[:, :], 1.0)

            @blk.sync
            def _(sy):
                cx.inc(sy.dma_start(out=scl_sb[0:1, 0:1], in_=T["g_scl"][0:1, 0:1]),
                       "cl4", 16)

        with nc.Block() as blk:
            @blk.scalar
            def _(s):
                cx.w(s, "cl4", 16)
                ins = s.activation(scl_hl[0:1, 0:1], scl_sb[0:1, 0:1], AF.Copy)
                cx.inc(ins, "shl4", 1)

            @blk.vector
            def _(v):
                cx.w(v, "shl4", 1)
                ins = v.tensor_copy(scl_sb[0:1, 1:2], scl_hl[0:1, 0:1])
                cx.fence(v, ins, "fv")
                ins = v.tensor_sub(scl_sb[0:1, 2:3], scl_sb[0:1, 0:1],
                                   scl_sb[0:1, 1:2])
                cx.fence(v, ins, "fv")
                ins = v.tensor_copy(scl_hl[0:1, 1:2], scl_sb[0:1, 2:3])
                cx.inc(ins, "shlv", 1)

            @blk.tensor
            def _(t):
                cx.w(t, "shlv", 1)
                t.matmul(p_b[:, 0:1], ones_rb[0:1, :], scl_hl[0:1, 0:1],
                         start=True, stop=False)
                ins = t.matmul(p_b[:, 0:1], ones_rb[0:1, :], scl_hl[0:1, 1:2],
                               start=False, stop=True)
                cx.inc(ins, "sb4", 1)

            @blk.vector
            def _(v):
                cx.w(v, "sb4", 1)
                cx.inc(v.tensor_copy(scl_bc[:, :], p_b[:, 0:1]), "sbv4", 1)

        for tc in range(8):
            bT = {k: cx.val(k) for k in
                  ("xl4", "wfl", "wpl", "pfc", "rle", "h2v", "ppj", "g2e",
                   "sgv4", "tp4", "te4", "x1l", "x2v", "oy", "id_v", "sbv4")}
            if DEV_DUMP and tc == 1:
                with nc.Block() as blk:
                    @blk.gpsimd
                    def _(g):
                        dbg = T["g_dbg"]
                        cx.inc(g.dma_start(
                            out=_ap(dbg, 0, [[NHF * 512, 128], [1, NHF * 512]]),
                            in_=h2[:, :]), "dbg4", 16)
                        cx.inc(g.dma_start(
                            out=_ap(dbg, 3000000, [[TS * D, 128], [1, TS * D]]),
                            in_=sgt[:, :]), "dbg4", 16)
                        cx.inc(g.dma_start(out=_ap(dbg, 3900000, [[1, 128], [1, 1]]),
                                           in_=scl_bc[:, :]), "dbg4", 16)
                        cx.inc(g.dma_start(out=_ap(dbg, 3910000, [[1024, 128], [1, 1024]]),
                                           in_=g2_sb[:, :]), "dbg4", 16)
                        g.wait_ge(cx.sem("dbg4"), cx.c["dbg4"])
            with nc.Block() as blk:
                @blk.sync
                def _(sy):
                    cx.w(sy, "oy", bT["oy"])
                    cx.inc(sy.dma_start(
                        out=xn2c[:, :],
                        in_=_ap(T["g_xn2"], tc * 512,
                                [[S, 128], [128 * S, KD], [1, 512]])), "xl4", 16)
                    cx.inc(sy.dma_start(
                        out=s2c[:, :],
                        in_=_ap(T["g_s2"], tc * 512,
                                [[S, 128], [128 * S, KD], [1, 512]])), "xl4", 16)
                    for mi in range(NHF):
                        if mi >= 2:
                            cx.w(sy, "h2v", bT["h2v"] + mi - 1)
                        cx.inc(sy.dma_start(
                            out=wfc_w[:, (mi % 2) * 2048:(mi % 2) * 2048 + 2048],
                            in_=_ap(T["g_q01fc"], mi * 128 * KD * 128,
                                    [[KD * 128, 128], [1, KD * 128]])), "wfl", 16)
                    for mi2 in range(16):
                        if mi2 >= 2:
                            cx.w(sy, "g2e", bT["g2e"] + mi2 - 1)
                        cx.inc(sy.dma_start(
                            out=wpj_w[:, (mi2 % 2) * 5120:(mi2 % 2) * 5120 + 5120],
                            in_=_ap(T["g_q01pj"], mi2 * 128 * NHF * 128,
                                    [[NHF * 128, 128], [1, NHF * 128]])), "wpl", 16)
                    for tsub in range(TS):
                        if tsub >= 2:
                            cx.w(sy, "x2v", bT["x2v"] + tsub - 1)
                            cx.w(sy, "oy", bT["oy"] + 16 * (tsub - 1))
                        cx.inc(sy.dma_start(
                            out=x1t[:, (tsub % 2) * D:(tsub % 2) * D + D],
                            in_=_ap(T["g_x1"], (tc * 512 + tsub * 128) * D,
                                    [[D, 128], [1, D]])), "x1l", 16)

                @blk.tensor
                def _(t):
                    cx.w(t, "xl4", bT["xl4"] + 32)
                    cx.w(t, "sbv4", 1)
                    cx.w(t, "id_v", cx.val("id_v"))
                    for mi in range(NHF):
                        sl = (mi % 2) * 2048
                        cx.w(t, "wfl", bT["wfl"] + 16 * (mi + 1))
                        if mi >= 2:
                            cx.w(t, "h2v", bT["h2v"] + mi - 1)
                        for ki in range(KD):
                            ins = t.matmul(
                                p_a[mi % 2][:, :],
                                wfc_w[:, sl + ki * 128:sl + ki * 128 + 128],
                                xn2c[:, ki * 512:ki * 512 + 512],
                                start=(ki == 0), stop=(ki == KD - 1))
                        cx.inc(ins, "pfc", 1)
                    for mi2 in range(16):
                        sl = (mi2 % 2) * 5120
                        cx.w(t, "wpl", bT["wpl"] + 16 * (mi2 + 1))
                        cx.w(t, "h2v", bT["h2v"] + NHF)
                        if mi2 >= 2:
                            cx.w(t, "g2e", bT["g2e"] + mi2 - 1)
                        for ki in range(NHF):
                            ins = t.matmul(
                                p_a[mi2 % 2][:, :],
                                wpj_w[:, sl + ki * 128:sl + ki * 128 + 128],
                                h2[:, ki * 512:ki * 512 + 512],
                                start=(ki == 0), stop=(ki == NHF - 1))
                        cx.inc(ins, "ppj", 1)
                        if mi2 >= 1:
                            _p4_tp(cx, t, bT, mi2 - 1, sgt, g2_sb, ident, p_t)
                    _p4_tp(cx, t, bT, 15, sgt, g2_sb, ident, p_t)

                @blk.scalar
                def _(s):
                    for mi in range(NHF):
                        cx.w(s, "pfc", bT["pfc"] + mi + 1)
                        if mi >= 2:
                            cx.w(s, "h2v", bT["h2v"] + mi - 1)
                        ins = s.activation(rl_sb[:, (mi % 2) * 512:(mi % 2) * 512 + 512],
                                           p_a[mi % 2][:, :], AF.Relu)
                        cx.inc(ins, "rle", 1)
                    for mi2 in range(16):
                        cx.w(s, "ppj", bT["ppj"] + mi2 + 1)
                        if mi2 >= 2:
                            cx.w(s, "sgv4", bT["sgv4"] + mi2 - 1)
                            cx.w(s, "tp4", bT["tp4"] + (mi2 - 1) * TS)
                        ins = s.activation(g2_sb[:, (mi2 % 2) * 512:(mi2 % 2) * 512 + 512],
                                           p_a[mi2 % 2][:, :], AF.Silu,
                                           scale=scl_bc[:, 0:1])
                        cx.inc(ins, "g2e", 1)

                @blk.vector
                def _(v):
                    for mi in range(NHF):
                        cx.w(v, "rle", bT["rle"] + mi + 1)
                        ins = v.tensor_mul(h2[:, mi * 512:mi * 512 + 512],
                                           p_a[mi % 2][:, :],
                                           rl_sb[:, (mi % 2) * 512:(mi % 2) * 512 + 512])
                        cx.inc(ins, "h2v", 1)
                    for mi2 in range(16):
                        cx.w(v, "g2e", bT["g2e"] + mi2 + 1)
                        ins = v.tensor_mul(g2_sb[:, (mi2 % 2) * 512:(mi2 % 2) * 512 + 512],
                                           g2_sb[:, (mi2 % 2) * 512:(mi2 % 2) * 512 + 512],
                                           s2c[:, mi2 * 512:mi2 * 512 + 512])
                        cx.inc(ins, "sgv4", 1)
                        if mi2 >= 1:
                            _p4_te(cx, v, bT, mi2 - 1, sgt, p_t)
                    _p4_te(cx, v, bT, 15, sgt, p_t)
                    for tsub in range(TS):
                        cx.w(v, "te4", bT["te4"] + 64)
                        cx.w(v, "x1l", bT["x1l"] + 16 * (tsub + 1))
                        if tsub >= 2:
                            cx.w(v, "oy", bT["oy"] + 16 * (tsub - 1))
                        ins = v.tensor_add(x1t[:, (tsub % 2) * D:(tsub % 2) * D + D],
                                           x1t[:, (tsub % 2) * D:(tsub % 2) * D + D],
                                           sgt[:, tsub * D:tsub * D + D])
                        cx.inc(ins, "x2v", 1)

                @blk.gpsimd
                def _(g):
                    for tsub in range(TS):
                        cx.w(g, "x2v", bT["x2v"] + tsub + 1)
                        cx.inc(g.dma_start(
                            out=_ap(T["y"], (tc * 512 + tsub * 128) * D,
                                    [[D, 128], [1, D]]),
                            in_=x1t[:, (tsub % 2) * D:(tsub % 2) * D + D]), "oy", 16)

        with nc.Block() as blk:
            @blk.gpsimd
            def _(g):
                cx.wall(g, "oy")
        if own_cx:
            _clear_sems(nc, cx)


def _p4_tp(cx, t, bT, mi2, sgt, g2_sb, ident, p_t):
    cx.w(t, "sgv4", bT["sgv4"] + mi2 + 1)
    for tsub in range(TS):
        n = mi2 * TS + tsub
        if n >= 2:
            cx.w(t, "te4", bT["te4"] + n - 1)
        ins = t.transpose(p_t[n % 2][:, :],
                          g2_sb[:, (mi2 % 2) * 512 + tsub * 128:
                                (mi2 % 2) * 512 + tsub * 128 + 128], ident[:, :])
        cx.inc(ins, "tp4", 1)


def _p4_te(cx, v, bT, mi2, sgt, p_t):
    for tsub in range(TS):
        n = mi2 * TS + tsub
        cx.w(v, "tp4", bT["tp4"] + n + 1)
        ins = v.tensor_copy(sgt[:, tsub * D + mi2 * 128:tsub * D + mi2 * 128 + 128],
                            p_t[n % 2][:, :])
        cx.inc(ins, "te4", 1)


def _mk_dram(nc, name, shape, dtype, kind):
    return nc.dram_tensor(name, shape, dtype, kind=kind).ap()


SPEC_IN = {
    "x": ([S, D], F32),
    "Wqkv0": ([D, F0], F32), "bqkv0": ([F0], F32),
    "Wout0": ([D, D], F32), "bout0": ([D], F32),
    "Wr": ([D, 8], F32),
    "Weg": ([E, D, DE], F32), "Weu": ([E, D, DE], F32), "Wed": ([E, DE, D], F32),
    "Wsg": ([D, DSH], F32), "Wsu": ([D, DSH], F32), "Wsd": ([DSH, D], F32),
    "wqkv1": ([D, 3 * D], F32), "b1": ([3 * D], F32),
    "wout1": ([D, D], F32), "bo1": ([D], F32),
    "wfc": ([HF, D], F32), "wproj": ([D, HF], F32),
    "cosb": ([ROT, S], F32), "sinb": ([ROT, S], F32),
    "pmat": ([ROT, ROT], F32), "mask4": ([128, 4 * 512], F32),
}
SPEC_SCR = {
    "g_mw": ([MW_TOTAL], BF16),
    "g_q01fc": ([Q01FC_TOTAL], BF16), "g_q01pj": ([Q01PJ_TOTAL], BF16),
    "g_scl": ([1, 4], F32),
    "g_wv": ([128 * KD * 128], BF16),
    "g_ws": ([128 * D], BF16),
    "g_cb": ([128, 16], F32),
    "g_tmp": ([8, 512], F32),
    "g_x1": ([S * D], F32), "g_xn2": ([D * S], BF16),
    "g_qk": ([2 * NH * 128 * S], BF16), "g_vtm": ([S * D], BF16),
    "g_s2": ([D * S], BF16),
    "g_dbg": ([1 << 23], F32),
}
SPEC_OUT = {"y": ([S, D], F32)}


def _run_fused(per_core, cores, inputs):
    nc = bass.Bass("TRN2", target_bir_lowering=False,
                   detect_race_conditions=False)
    T = {}
    for n in ALL_INS:
        T[n] = _mk_dram(nc, n, *SPEC_IN[n], kind="ExternalInput")
    for n, (shape, dt) in SPEC_SCR.items():
        T[n] = _mk_dram(nc, n, shape, dt, kind="Internal")
    for n, (shape, dt) in SPEC_OUT.items():
        T[n] = _mk_dram(nc, n, shape, dt, kind="ExternalOutput")
    emit_prep(nc, T)
    emit_p1(nc, T)
    emit_p2(nc, T)
    emit_p3(nc, T)
    emit_p4(nc, T)
    in_maps = [{k: per_core[b][k] for k in ALL_INS} for b in range(8)]
    r = run_bass_kernel_spmd(nc, in_maps, core_ids=cores)
    out = np.stack([r.results[b]["y"].reshape(S, D) for b in range(8)])
    return out.astype(np.float32)


def _build_phase(emit_fns, ins, scr_in, scr_out, outs):
    nc = bass.Bass("TRN2", target_bir_lowering=False, detect_race_conditions=False)
    T = {}
    for n in ins:
        T[n] = _mk_dram(nc, n, *SPEC_IN[n], kind="ExternalInput")
    for n, (shape, dt) in SPEC_SCR.items():
        if n in scr_in:
            T[n] = _mk_dram(nc, n, shape, dt, kind="ExternalInput")
        elif n in scr_out:
            T[n] = _mk_dram(nc, n, shape, dt, kind="ExternalOutput")
        else:
            T[n] = _mk_dram(nc, n, shape, dt, kind="Internal")
    for n, (shape, dt) in SPEC_OUT.items():
        if n in outs:
            T[n] = _mk_dram(nc, n, shape, dt, kind="ExternalOutput")
    for fn in emit_fns:
        fn(nc, T)
    return nc


def _np_inputs(inputs):
    f = np.ascontiguousarray
    base = {
        "Wqkv0": f(inputs["L0_attn_Wqkv"].reshape(D, F0)),
        "bqkv0": f(inputs["L0_attn_bqkv"].reshape(F0)),
        "Wout0": f(inputs["L0_attn_Wout"].reshape(D, D)),
        "bout0": f(inputs["L0_attn_bout"]),
        "Wr": f(inputs["L0_ffn_W_router"]),
        "Weg": f(inputs["L0_ffn_We_gate"]),
        "Weu": f(inputs["L0_ffn_We_up"]),
        "Wed": f(inputs["L0_ffn_We_down"]),
        "Wsg": f(inputs["L0_ffn_Ws_gate"]),
        "Wsu": f(inputs["L0_ffn_Ws_up"]),
        "Wsd": f(inputs["L0_ffn_Ws_down"]),
        "wqkv1": f(inputs["L1_attn_w_qkv"]),
        "b1": f(inputs["L1_attn_b_qkv"]),
        "wout1": f(inputs["L1_attn_w_out"]),
        "bo1": f(inputs["L1_attn_b_out"]),
        "wfc": f(inputs["L1_ffn_w_fc"]),
        "wproj": f(inputs["L1_ffn_w_proj"]),
    }
    inv = 1.0 / (10000.0 ** (np.arange(0, ROT, 2, dtype=np.float64) / ROT))
    tpos = np.arange(S, dtype=np.float64)
    fr = np.concatenate([np.outer(inv, tpos), np.outer(inv, tpos)], axis=0)
    base["cosb"] = np.cos(fr).astype(np.float32)
    base["sinb"] = np.sin(fr).astype(np.float32)
    pm = np.zeros((ROT, ROT), dtype=np.float32)
    for mrow in range(16):
        pm[mrow + 16, mrow] = -1.0
        pm[mrow, mrow + 16] = 1.0
    base["pmat"] = pm
    msk = np.zeros((128, 4 * 512), dtype=np.float32)
    for r in range(4):
        for i in range(128):
            msk[i, r * 512:(r + 1) * 512] = np.where(
                np.arange(512) >= r * 128 + i, 0.0, -1.0e9)
    base["mask4"] = msk
    per_core = []
    for b in range(8):
        m = dict(base)
        m["x"] = f(inputs["L0_attn_x"][b])
        per_core.append(m)
    return per_core


FUSED = True

ALL_INS = ["x", "Wqkv0", "bqkv0", "Wout0", "bout0", "Wr", "Weg", "Weu", "Wed",
           "Wsg", "Wsu", "Wsd", "wqkv1", "b1", "wout1", "bo1", "wfc", "wproj",
           "cosb", "sinb", "pmat", "mask4"]


def kernel(**inputs):
    per_core = _np_inputs(inputs)
    cores = list(range(8))
    if FUSED:
        return _run_fused(per_core, cores, inputs)

    nc = _build_phase([emit_prep],
                      ins=["Wqkv0", "Wout0", "Wr", "Weg", "Weu", "Wed", "Wsg",
                           "Wsu", "Wsd", "wfc", "wproj", "bqkv0", "bout0"],
                      scr_in=set(),
                      scr_out={"g_mw", "g_q01fc", "g_q01pj", "g_scl", "g_wv",
                               "g_ws", "g_cb"},
                      outs=set())
    in_maps = [{k: per_core[b][k] for k in
                ("Wqkv0", "Wout0", "Wr", "Weg", "Weu", "Wed", "Wsg", "Wsu",
                 "Wsd", "wfc", "wproj", "bqkv0", "bout0")} for b in range(8)]
    r1 = run_bass_kernel_spmd(nc, in_maps, core_ids=cores)

    nc = _build_phase([emit_p1],
                      ins=["x"],
                      scr_in={"g_mw", "g_wv", "g_ws", "g_cb"},
                      scr_out={"g_x1", "g_xn2"}, outs=set())
    in_maps = []
    for b in range(8):
        m = {"x": per_core[b]["x"]}
        for kk in ("g_mw", "g_wv", "g_ws", "g_cb"):
            m[kk] = r1.results[b][kk]
        in_maps.append(m)
    r2 = run_bass_kernel_spmd(nc, in_maps, core_ids=cores)

    nc = _build_phase([emit_p2],
                      ins=["wqkv1", "b1", "cosb", "sinb", "pmat"],
                      scr_in={"g_xn2"}, scr_out={"g_qk", "g_vtm"}, outs=set())
    in_maps = []
    for b in range(8):
        m = {k: per_core[b][k] for k in ("wqkv1", "b1", "cosb", "sinb", "pmat")}
        m["g_xn2"] = r2.results[b]["g_xn2"]
        in_maps.append(m)
    r3 = run_bass_kernel_spmd(nc, in_maps, core_ids=cores)

    nc = _build_phase([emit_p3],
                      ins=["wout1", "bo1", "mask4"],
                      scr_in={"g_qk", "g_vtm"}, scr_out={"g_s2"}, outs=set())
    in_maps = []
    for b in range(8):
        m = {k: per_core[b][k] for k in ("wout1", "bo1", "mask4")}
        m["g_qk"] = r3.results[b]["g_qk"]
        m["g_vtm"] = r3.results[b]["g_vtm"]
        in_maps.append(m)
    r4 = run_bass_kernel_spmd(nc, in_maps, core_ids=cores)

    nc = _build_phase([emit_p4], ins=[],
                      scr_in={"g_xn2", "g_x1", "g_s2", "g_q01fc", "g_q01pj",
                              "g_scl"},
                      scr_out=set(), outs={"y"})
    in_maps = []
    for b in range(8):
        m = {"g_xn2": r2.results[b]["g_xn2"], "g_x1": r2.results[b]["g_x1"],
             "g_s2": r4.results[b]["g_s2"], "g_q01fc": r1.results[b]["g_q01fc"],
             "g_q01pj": r1.results[b]["g_q01pj"], "g_scl": r1.results[b]["g_scl"]}
        in_maps.append(m)
    r5 = run_bass_kernel_spmd(nc, in_maps, core_ids=cores)

    out = np.stack([r5.results[b]["y"].reshape(S, D) for b in range(8)])
    return out.astype(np.float32)


# version 08 (final)
# speedup vs baseline: 1.0148x; 1.0148x over previous
import numpy as np
import ml_dtypes
from contextlib import ExitStack
import concourse.bass as bass
import concourse.mybir as mybir
from concourse.bass_utils import run_bass_kernel_spmd

F32 = mybir.dt.float32
BF16 = mybir.dt.bfloat16
AF = mybir.ActivationFunctionType
ALU = mybir.AluOpType
AX = mybir.AxisListType

S, D = 4096, 2048
CH, NCH, TS = 512, 8, 4
KD = D // 128
E, DE, NDE = 8, 1408, 11
DSH, NDS = 2816, 22
F0 = 6144
NH, DH, ROT = 16, 128, 32
HF, NHF = 5120, 40
LN_EPS = 1.0e-5
RS128 = 0.08838834764831845

DEV_DUMP = False
DEV_DUMP_CHUNK = 0
PAIRS = [(i, j) for i in range(16) for j in range(i + 1)]
POFF = [i * (i + 1) // 2 for i in range(16)]


def _mw_layout():
    off, o = {}, 0
    off["wqkv0"] = o; o += 12 * 128 * KD * 512
    off["wout0"] = o; o += 16 * 128 * KD * 128
    off["router"] = o; o += 128 * KD * 8
    off["gate"] = o; o += E * NDE * 128 * KD * 128
    off["up"] = o; o += E * NDE * 128 * KD * 128
    off["down"] = o; o += E * 16 * 128 * NDE * 128
    off["sg"] = o; o += NDS * 128 * KD * 128
    off["su"] = o; o += NDS * 128 * KD * 128
    off["sd"] = o; o += 16 * 128 * NDS * 128
    return off, o


MW_OFF, MW_TOTAL = _mw_layout()
Q01FC_TOTAL = NHF * 128 * KD * 128
Q01PJ_TOTAL = KD * 128 * NHF * 128


def _ap(t, off, dims):
    return bass.AP(t.tensor if isinstance(t, bass.AP) else t, off, dims)


class Ctx:
    def __init__(self, nc, stack):
        self.nc, self.stack, self.h, self.c = nc, stack, {}, {}

    def sem(self, name):
        if name not in self.h:
            self.h[name] = self.stack.enter_context(self.nc.semaphore(name))
            self.c[name] = 0
        return self.h[name]

    def inc(self, instr, name, n=1):
        instr.then_inc(self.sem(name), n)
        self.c[name] += n
        return self.c[name]

    def w(self, eng, name, val):
        eng.wait_ge(self.sem(name), val)

    def wall(self, eng, name):
        eng.wait_ge(self.sem(name), self.c[name])

    def val(self, name):
        self.sem(name)
        return self.c[name]

    def fence(self, eng, ins, tag):
        n = self.inc(ins, tag)
        eng.wait_ge(self.h[tag], n)


def _clear_sems(nc, cx):
    with nc.Block() as blk:
        @blk.sync
        def _(sy):
            for nm in list(cx.h):
                sy.sem_clear(cx.h[nm])


def _emit_ident(nc, cx, ident, iota_scr):
    with nc.Block() as blk:
        @blk.gpsimd
        def _(g):
            ins = g.iota(iota_scr[:, :], [[1, 128]], base=0, channel_multiplier=-1,
                         allow_small_or_imprecise_dtypes=True)
            cx.inc(ins, "id_g", 1)

        @blk.vector
        def _(v):
            cx.w(v, "id_g", cx.val("id_g"))
            ins = v.tensor_single_scalar(ident[:, :], iota_scr[:, :], 0.0, ALU.is_equal)
            cx.inc(ins, "id_v", 1)


PREP_PARTS = {"dma", "stats", "scal", "qfc", "qpj"}


def emit_prep(nc, T, cx=None):
    with ExitStack() as st:
        own_cx = cx is None
        cx = cx if cx is not None else Ctx(nc, st)
        sb = lambda nm, shp, dt: st.enter_context(nc.sbuf_tensor(nm, shp, dt))
        mw = T["g_mw"]

        wstat = sb("pp_wstat", [128, 64], F32)
        wtile = sb("pp_wtile", [128, 2 * HF], F32)
        sc1 = sb("pp_sc1", [1, 16], F32)
        scb = sb("pp_scb", [128, 4], F32)
        ones_r = sb("pp_ones", [1, 128], F32)
        ones_rb = sb("pp_onesb", [1, 128], BF16)
        ones_c = sb("pp_onec", [128, 1], F32)
        ident = sb("pp_id", [128, 128], BF16)
        iota_s = sb("pp_io", [128, 128], F32)
        qsgn = sb("pp_sgn", [128, 2 * HF], BF16)
        qabs = sb("pp_abs", [128, 2 * HF], F32)
        qm01 = sb("pp_m01", [128, HF], BF16)
        q01 = sb("pp_q01", [128, HF], BF16)
        qrow = sb("pp_qrow", [128, HF], BF16)
        pt = [st.enter_context(nc.psum_tensor(f"pp_pt{i}", [128, 128], BF16))
              for i in range(2)]
        ps = st.enter_context(nc.psum_tensor("pp_ps", [128, 4], F32))
        pp_cb = st.enter_context(nc.psum_tensor("pp_cb", [128, 16], F32))

        _emit_ident(nc, cx, ident, iota_s)

        with nc.Block() as blk:
            @blk.gpsimd
            def _(g):
                g.memset(ones_r[:, :], 1.0)
                g.memset(ones_rb[:, :], 1.0)
                g.memset(ones_c[:, :], 1.0)
                g.memset(qm01[64:80, 384:385], 1.0)
                cx.inc(g.dma_start(out=qm01[64:80, 0:384],
                                   in_=T["bqkv0"][None, :].rearrange(
                                       "o (h f) -> (o h) f", f=384)), "bql", 16)
                cx.inc(g.dma_start(out=qm01[32:48, 0:128],
                                   in_=_ap(T["bout0"], 0, [[128, 16], [1, 128]])),
                       "bql", 16)
                if "dma" not in PREP_PARTS:
                    return
                dst = _ap(mw, MW_OFF["router"], [[KD * 8, 128], [8, KD], [1, 8]])
                src = _ap(T["Wr"], 0, [[8, 128], [128 * 8, KD], [1, 8]])
                cx.inc(g.dma_start(out=dst, in_=src), "pdma", 16)
                for nm, wt in (("gate", T["Weg"]), ("up", T["Weu"])):
                    for e in range(E):
                        for mi in range(NDE):
                            dst = _ap(mw, MW_OFF[nm] + (e * NDE + mi) * 128 * KD * 128,
                                      [[KD * 128, 128], [128, KD], [1, 128]])
                            src = _ap(wt, e * D * DE + mi * 128,
                                      [[DE, 128], [128 * DE, KD], [1, 128]])
                            cx.inc(g.dma_start(out=dst, in_=src), "pdma", 16)
                for e in range(E):
                    for mi in range(16):
                        dst = _ap(mw, MW_OFF["down"] + (e * 16 + mi) * 128 * NDE * 128,
                                  [[NDE * 128, 128], [128, NDE], [1, 128]])
                        src = _ap(T["Wed"], e * DE * D + mi * 128,
                                  [[D, 128], [128 * D, NDE], [1, 128]])
                        cx.inc(g.dma_start(out=dst, in_=src), "pdma", 16)
                for nm, wt in (("sg", T["Wsg"]), ("su", T["Wsu"])):
                    for mi in range(NDS):
                        dst = _ap(mw, MW_OFF[nm] + mi * 128 * KD * 128,
                                  [[KD * 128, 128], [128, KD], [1, 128]])
                        src = _ap(wt, mi * 128, [[DSH, 128], [128 * DSH, KD], [1, 128]])
                        cx.inc(g.dma_start(out=dst, in_=src), "pdma", 16)
                for mi in range(16):
                    dst = _ap(mw, MW_OFF["sd"] + mi * 128 * NDS * 128,
                              [[NDS * 128, 128], [128, NDS], [1, 128]])
                    src = _ap(T["Wsd"], mi * 128, [[D, 128], [128 * D, NDS], [1, 128]])
                    cx.inc(g.dma_start(out=dst, in_=src), "pdma", 16)

        with nc.Block() as blk:
            @blk.sync
            def _(sy):
                if "stats" not in PREP_PARTS:
                    return
                for r in range(NHF):
                    if r >= 2:
                        cx.w(sy, "sred", r - 1)
                    cx.inc(sy.dma_start(out=wtile[:, (r % 2) * D:(r % 2) * D + D],
                                        in_=_ap(T["wfc"], r * 128 * D, [[D, 128], [1, D]])),
                           "sdma", 16)
                for r2 in range(KD):
                    cx.w(sy, "sred", NHF + max(r2 - 1, 0))
                    cx.inc(sy.dma_start(out=wtile[:, (r2 % 2) * HF:(r2 % 2) * HF + HF],
                                        in_=_ap(T["wproj"], r2 * 128 * HF,
                                                [[HF, 128], [1, HF]])), "sdma", 16)

            @blk.vector
            def _(v):
                if "stats" not in PREP_PARTS:
                    return
                for r in range(NHF):
                    cx.w(v, "sdma", 16 * (r + 1))
                    ins = v.reduce_sum(wstat[:, r:r + 1],
                                       wtile[:, (r % 2) * D:(r % 2) * D + D],
                                       AX.X, apply_absolute_value=True)
                    cx.inc(ins, "sred", 1)
                for r2 in range(KD):
                    cx.w(v, "sdma", 16 * (NHF + r2 + 1))
                    ins = v.reduce_sum(wstat[:, 40 + r2:41 + r2],
                                       wtile[:, (r2 % 2) * HF:(r2 % 2) * HF + HF],
                                       AX.X, apply_absolute_value=True)
                    cx.inc(ins, "sred", 1)
                cx.wall(v, "sred")
                v.reduce_sum(wstat[:, 60:61], wstat[:, 0:NHF], AX.X)
                ins = v.reduce_sum(wstat[:, 61:62], wstat[:, 40:40 + KD], AX.X)
                cx.inc(ins, "svec", 1)

        if "scal" not in PREP_PARTS:
            return
        b2 = {"svec": cx.val("svec")}
        with nc.Block() as blk:
            @blk.sync
            def _(sy):
                cx.inc(sy.dma_start(out=_ap(T["g_tmp"], 0, [[1, 128], [128, 2]]),
                                    in_=wstat[:, 60:62],
                                    allow_slow_non_contiguous=True), "bnc", 16)
                cx.w(sy, "bnc", 16)
                cx.inc(sy.dma_start(out=wtile[0:1, 0:256],
                                    in_=_ap(T["g_tmp"], 0, [[256, 1], [1, 256]])),
                       "bnc", 16)

            @blk.vector
            def _(v):
                cx.w(v, "bnc", 32)
                v.reduce_sum(sc1[0:1, 5:6], wtile[0:1, 0:128], AX.X)
                ins = v.reduce_sum(sc1[0:1, 6:7], wtile[0:1, 128:256], AX.X)
                cx.fence(v, ins, "fv")
                v.tensor_scalar(sc1[0:1, 0:1], sc1[0:1, 5:6], 1.0 / (HF * D), 1.0e-5,
                                ALU.mult, ALU.max)
                ins = v.tensor_scalar(sc1[0:1, 1:2], sc1[0:1, 6:7], 1.0 / (HF * D),
                                      1.0e-5, ALU.mult, ALU.max)
                cx.fence(v, ins, "fv")
                ins = v.tensor_scalar_mul(sc1[0:1, 2:4], sc1[0:1, 0:2], 0.7)
                ins = v.tensor_mul(sc1[0:1, 4:5], sc1[0:1, 0:1], sc1[0:1, 0:1])
                cx.fence(v, ins, "fv")
                ins = v.tensor_mul(sc1[0:1, 4:5], sc1[0:1, 4:5], sc1[0:1, 1:2])
                cx.inc(ins, "svec", 1)

            @blk.scalar
            def _(s):
                cx.w(s, "svec", b2["svec"] + 1)
                ins = s.activation(qm01[0:1, 0:3], sc1[0:1, 2:5], AF.Copy)
                cx.inc(ins, "shl", 1)

            @blk.vector
            def _(v):
                cx.w(v, "shl", 1)
                ins = v.tensor_copy(sc1[0:1, 8:11], qm01[0:1, 0:3])
                cx.fence(v, ins, "fv")
                ins = v.tensor_sub(sc1[0:1, 11:14], sc1[0:1, 2:5], sc1[0:1, 8:11])
                cx.fence(v, ins, "fv")
                ins = v.tensor_copy(qm01[0:1, 4:7], sc1[0:1, 11:14])
                cx.inc(ins, "svec", 1)

            @blk.tensor
            def _(t):
                cx.w(t, "id_v", cx.val("id_v"))
                cx.w(t, "svec", b2["svec"] + 2)
                t.matmul(ps[:, 0:3], ones_rb[0:1, :], qm01[0:1, 0:3],
                         start=True, stop=False)
                ins = t.matmul(ps[:, 0:3], ones_rb[0:1, :], qm01[0:1, 4:7],
                               start=False, stop=True)
                cx.inc(ins, "pmm", 1)

            @blk.vector
            def _(v):
                cx.w(v, "pmm", 1)
                cx.inc(v.tensor_copy(scb[:, 0:3], ps[:, 0:3]), "svec", 1)

            @blk.sync
            def _(sy):
                cx.w(sy, "svec", cx.val("svec"))
                cx.inc(sy.dma_start(out=T["g_scl"][0:1, 0:1], in_=scb[0:1, 2:3]),
                       "odma", 16)

        vacc = wtile
        with nc.Block() as blk:
            @blk.sync
            def _(sy):
                for h in range(16):
                    if h >= 2:
                        cx.w(sy, "vac", h - 1)
                    cx.inc(sy.dma_start(
                        out=vacc[:, 2 * D + (h % 2) * D:2 * D + (h % 2) * D + D],
                        in_=_ap(T["Wqkv0"], h * 384 + 256,
                                [[F0, 128], [128 * F0, KD], [1, 128]])), "vld", 16)
                for h in range(16):
                    cx.w(sy, "vac", (16 + h) if h >= 2 else 17)
                    cx.inc(sy.dma_start(
                        out=vacc[:, 2 * D + (h % 2) * D:2 * D + (h % 2) * D + D],
                        in_=_ap(T["Wout0"], h * 128 * D, [[D, 128], [1, D]])),
                        "vld", 16)

            @blk.vector
            def _(v):
                for h in range(16):
                    cx.w(v, "vld", 16 * (h + 1))
                    if h == 0:
                        ins = v.tensor_copy(vacc[:, 0:D],
                                            vacc[:, 2 * D:3 * D])
                    else:
                        ins = v.tensor_add(vacc[:, 0:D], vacc[:, 0:D],
                                           vacc[:, 2 * D + (h % 2) * D:
                                                2 * D + (h % 2) * D + D])
                    cx.inc(ins, "vac", 1)
                    cx.wall(v, "vac")
                ins = v.tensor_copy(qrow[:, 0:D], vacc[:, 0:D])
                cx.inc(ins, "vac", 1)
                for h in range(16):
                    cx.w(v, "vld", 16 * (16 + h + 1))
                    if h == 0:
                        ins = v.tensor_copy(vacc[:, D:2 * D], vacc[:, 2 * D:3 * D])
                    else:
                        ins = v.tensor_add(vacc[:, D:2 * D], vacc[:, D:2 * D],
                                           vacc[:, 2 * D + (h % 2) * D:
                                                2 * D + (h % 2) * D + D])
                    cx.inc(ins, "vac", 1)
                    cx.wall(v, "vac")
                ins = v.tensor_copy(qrow[:, D:2 * D], vacc[:, D:2 * D])
                cx.inc(ins, "wsb", 1)

            @blk.sync
            def _(sy):
                cx.w(sy, "vac", 17 + 16)
                cx.inc(sy.dma_start(out=_ap(T["g_wv"], 0, [[D, 128], [1, D]]),
                                    in_=qrow[:, 0:D]), "vout", 16)
                cx.w(sy, "wsb", 1)
                cx.inc(sy.dma_start(out=_ap(T["g_ws"], 0, [[D, 128], [1, D]]),
                                    in_=qrow[:, D:2 * D]), "vout", 16)


        with nc.Block() as blk:
            @blk.tensor
            def _(t):
                cx.w(t, "bql", 32)
                cx.w(t, "id_v", cx.val("id_v"))
                cx.w(t, "wsb", 1)
                ins = t.matmul(ps[:, 0:1], qm01[64:80, 256:384],
                               qm01[64:80, 384:385], start=True, stop=True)
                cx.inc(ins, "bvp", 1)
                ins = t.transpose(pt[1][:, 0:16], qm01[32:48, 0:128],
                                  ident[32:48, 32:48])
                cx.inc(ins, "bvp", 1)

            @blk.vector
            def _(v):
                cx.w(v, "bvp", 2)
                ins = v.tensor_copy(q01[:, 0:1], ps[:, 0:1])
                cx.inc(ins, "bvv", 1)

            @blk.tensor
            def _(t):
                cx.w(t, "bvv", 1)
                for mi in range(16):
                    ins = t.matmul(ps[:, 0:1] if False else pp_cb[:, mi:mi + 1],
                                   qrow[:, D + mi * 128:D + mi * 128 + 128],
                                   q01[:, 0:1], start=True, stop=True)
                cx.inc(ins, "cbp", 1)

            @blk.vector
            def _(v):
                cx.w(v, "cbp", 1)
                ins = v.tensor_copy(wstat[:, 0:16], pt[1][:, 0:16])
                cx.fence(v, ins, "fv")
                ins = v.tensor_add(wstat[:, 16:32], pp_cb[:, 0:16], wstat[:, 0:16])
                cx.inc(ins, "cbv", 1)

            @blk.sync
            def _(sy):
                cx.w(sy, "cbv", 1)
                cx.inc(sy.dma_start(out=T["g_cb"][:, :], in_=wstat[:, 16:32]),
                       "vout", 16)
                cx.wall(sy, "vout")

        def q_pass(which):
            if which == "fc":
                ntile, width, thr_col, noutk = NHF, D, 0, KD
                gdst, src = T["g_q01fc"], T["wfc"]
            else:
                ntile, width, thr_col, noutk = KD, HF, 1, NHF
                gdst, src = T["g_q01pj"], T["wproj"]
            qd, qs, qv, qt, qe, qo = [x + which for x in
                                      ("qd", "qs", "qv", "qt", "qe", "qo")]
            with nc.Block() as blk:
                @blk.sync
                def _(sy):
                    for r in range(ntile + 3):
                        if r < ntile:
                            if r >= 2:
                                cx.w(sy, qs, 2 * (r - 1))
                            cx.inc(sy.dma_start(
                                out=wtile[:, (r % 2) * width:(r % 2) * width + width],
                                in_=_ap(src, r * 128 * width,
                                        [[width, 128], [1, width]])), qd, 16)
                        if r >= 3:
                            r2 = r - 3
                            cx.w(sy, qe, (r2 + 1) * noutk)
                            cx.inc(sy.dma_start(
                                out=_ap(gdst, r2 * 128 * noutk * 128,
                                        [[noutk * 128, 128], [1, noutk * 128]]),
                                in_=qrow[:, 0:noutk * 128]), qo, 16)

                @blk.scalar
                def _(s):
                    for r in range(ntile):
                        cx.w(s, qd, 16 * (r + 1))
                        if r >= 2:
                            cx.w(s, qv, r - 1)
                        wt = wtile[:, (r % 2) * width:(r % 2) * width + width]
                        sl = (r % 2) * HF
                        ins = s.activation(qsgn[:, sl:sl + width], wt, AF.Sign)
                        cx.inc(ins, qs, 1)
                        ins = s.activation(qabs[:, sl:sl + width], wt, AF.Abs)
                        cx.inc(ins, qs, 1)

                @blk.vector
                def _(v):
                    def evacs(r):
                        for cb in range(noutk):
                            cx.w(v, qt, r * noutk + cb + 1)
                            ins = v.tensor_copy(qrow[:, cb * 128:cb * 128 + 128],
                                                pt[cb % 2][:, :])
                            cx.inc(ins, qe, 1)
                    for r in range(ntile):
                        cx.w(v, qs, 2 * (r + 1))
                        sl = (r % 2) * HF
                        ins = v.tensor_single_scalar(qm01[:, 0:width],
                                                     qabs[:, sl:sl + width],
                                                     scb[:, thr_col:thr_col + 1],
                                                     ALU.is_gt)
                        cx.fence(v, ins, "fv")
                        if r >= 1:
                            cx.w(v, qt, r * noutk)
                        ins = v.tensor_mul(q01[:, 0:width], qm01[:, 0:width],
                                           qsgn[:, sl:sl + width])
                        cx.inc(ins, qv, 1)
                        evacs(r)

                @blk.tensor
                def _(t):
                    for r in range(ntile):
                        cx.w(t, qv, r + 1)
                        if r >= 1:
                            cx.w(t, qo, 16 * r)
                        for cb in range(noutk):
                            if r * noutk + cb >= 2:
                                cx.w(t, qe, r * noutk + cb - 1)
                            ins = t.transpose(pt[cb % 2][:, :],
                                              q01[:, cb * 128:cb * 128 + 128],
                                              ident[:, :])
                            cx.inc(ins, qt, 1)

        if "qfc" in PREP_PARTS:
            q_pass("fc")
        if "qpj" in PREP_PARTS:
            q_pass("pj")

        with nc.Block() as blk:
            @blk.sync
            def _(sy):
                cx.wall(sy, "pdma")
                cx.wall(sy, "odma")
                cx.wall(sy, "vout")
                cx.wall(sy, "qofc")
                cx.wall(sy, "qopj")
        if own_cx:
            _clear_sems(nc, cx)


def _prep_end():
    pass


def emit_p1(nc, T, cx=None):
    with ExitStack() as st:
        own_cx = cx is None
        cx = cx if cx is not None else Ctx(nc, st)
        sb = lambda nm, shp, dt: st.enter_context(nc.sbuf_tensor(nm, shp, dt))
        mw = T["g_mw"]

        ident = sb("p1_id", [128, 128], BF16)
        identf = sb("p1_idf", [128, 128], F32)
        iota_s = sb("p1_io", [128, 128], F32)
        ones_r = sb("p1_on", [1, 512], BF16)
        router_w = sb("p1_rw", [128, KD * 8], BF16)
        wv_sb = sb("p1_wv", [128, KD * 128], BF16)
        ws_sb = sb("p1_ws", [128, D], BF16)
        c_pt = sb("p1_cpt", [128, 16], F32)
        lnst = sb("p1_ln", [128, 16], F32)
        cst = sb("p1_cst8", [128, 8], F32)
        x_tile = sb("p1_x", [128, 2 * D], F32)
        xn_tm = sb("p1_xntm", [128, 2 * D], BF16)
        xn_fm = sb("p1_xnfm", [128, KD * CH], BF16)
        vs_t = sb("p1_vst", [128, 2 * 128], BF16)
        vsT = sb("p1_vsT", [128, 2 * 128], BF16)
        cstat = sb("p1_cst", [128, 24], F32)
        s_fm = sb("p1_sfm", [128, KD * CH], BF16)
        comb_tm = sb("p1_ctm", [128, TS * 8], F32)
        comb_T = sb("p1_cT", [8, CH], F32)
        comb_bc = sb("p1_cbc", [128, E * CH], BF16)
        h_buf = sb("p1_h", [128, NDE * CH], BF16)
        racc = sb("p1_ra", [128, KD * CH], F32)
        g_fm = xn_fm
        silu_t = sb("p1_si", [128, 2 * CH], BF16)
        h_t = sb("p1_ht", [128, 2 * CH], BF16)
        sg_tm = sb("p1_sg", [128, D], BF16)
        xn2f_t = sb("p1_x2f", [128, D], BF16)
        NSL, SLW = 3, 4096
        moe_w = sb("p1_wm", [128, NSL * SLW], BF16)
        comb_rows = h_buf

        p_a = [st.enter_context(nc.psum_tensor(f"p1_pa{i}", [128, 512], F32))
               for i in range(4)]
        p_t = [st.enter_context(nc.psum_tensor(f"p1_pt{i}", [128, 128], BF16))
               for i in range(2)]
        p_r8 = st.enter_context(nc.psum_tensor("p1_pr8", [128, 8], F32))
        p_c8 = st.enter_context(nc.psum_tensor("p1_pc8", [8, 512], F32))

        _emit_ident(nc, cx, ident, iota_s)
        with nc.Block() as blk:
            @blk.vector
            def _(v):
                cx.w(v, "id_v", cx.val("id_v"))
                ins = v.tensor_single_scalar(identf[:, :], iota_s[:, :], 0.0,
                                             ALU.is_equal)
                cx.inc(ins, "id_v", 1)

        with nc.Block() as blk:
            @blk.gpsimd
            def _(g):
                g.memset(ones_r[:, :], 1.0)
                g.memset(cst[:, 0:1], LN_EPS)
                g.memset(cst[:, 1:2], -0.5)
                g.memset(cst[:, 2:3], 1.0 / D)
                ins = g.memset(cst[:, 3:4], -1.0)
                cx.inc(ins, "ldm", 1)

            @blk.sync
            def _(sy):
                cx.inc(sy.dma_start(out=router_w[:, :],
                                    in_=_ap(mw, MW_OFF["router"],
                                            [[KD * 8, 128], [1, KD * 8]])),
                       "ldh", 16)
                cx.inc(sy.dma_start(out=wv_sb[:, :],
                                    in_=_ap(T["g_wv"], 0, [[D, 128], [1, D]])),
                       "ldh", 16)
                cx.inc(sy.dma_start(out=ws_sb[:, :],
                                    in_=_ap(T["g_ws"], 0, [[D, 128], [1, D]])),
                       "ldh", 16)
                cx.inc(sy.dma_start(out=c_pt[:, :], in_=T["g_cb"][:, :]),
                       "ldh", 16)

        def ln_scalar(s, x_in, xn_out, junk_bf, tag):
            s.activation(junk_bf, x_in, AF.Copy, accum_out=lnst[:, 0:1])
            ins = s.activation(junk_bf, x_in, AF.Square, accum_out=lnst[:, 1:2])
            cx.fence(s, ins, "fs")
            s.mul(lnst[:, 2:3], lnst[:, 0:1], cst[:, 2:3])
            ins = s.mul(lnst[:, 3:4], lnst[:, 1:2], cst[:, 2:3])
            cx.fence(s, ins, "fs")
            ins = s.activation(lnst[:, 6:7], lnst[:, 2:3], AF.Square)
            cx.fence(s, ins, "fs")
            ins = s.mul(lnst[:, 6:7], lnst[:, 6:7], cst[:, 3:4])
            cx.fence(s, ins, "fs")
            ins = s.add(lnst[:, 3:4], lnst[:, 3:4], lnst[:, 6:7])
            cx.fence(s, ins, "fs")
            ins = s.activation(lnst[:, 7:8], lnst[:, 3:4], AF.Ln, bias=cst[:, 0:1])
            cx.fence(s, ins, "fs")
            ins = s.activation(lnst[:, 4:5], lnst[:, 7:8], AF.Exp, scale=cst[:, 1:2])
            cx.fence(s, ins, "fs")
            ins = s.mul(lnst[:, 5:6], lnst[:, 2:3], lnst[:, 4:5])
            cx.fence(s, ins, "fs")
            ins = s.mul(lnst[:, 5:6], lnst[:, 5:6], cst[:, 3:4])
            cx.fence(s, ins, "fs")
            ins = s.activation(xn_out, x_in, AF.Identity,
                               bias=lnst[:, 5:6], scale=lnst[:, 4:5])
            cx.inc(ins, tag, 1)

        ring = {"n": 0}
        unit_free = []

        def units_for(e):
            units = []
            if e < E:
                for i in range(0, NDE, 2):
                    n = 2 if i + 1 < NDE else 1
                    for nm, kind in (("gate", "g"), ("up", "u")):
                        off = MW_OFF[nm] + (e * NDE + i) * 128 * KD * 128
                        units.append((off, KD * 128, n, KD * 128,
                                      [(kind, i + k, k * KD * 128) for k in range(n)]))
                for i in range(0, 16, 2):
                    off = MW_OFF["down"] + (e * 16 + i) * 128 * NDE * 128
                    units.append((off, NDE * 128, 2, NDE * 128,
                                  [("d", i + k, k * NDE * 128) for k in range(2)]))
            else:
                for half in range(2):
                    for i in range(0, 11, 2):
                        n = 2 if i + 1 < 11 else 1
                        mi0 = half * 11 + i
                        for nm, kind in (("sg", "g"), ("su", "u")):
                            off = MW_OFF[nm] + mi0 * 128 * KD * 128
                            units.append((off, KD * 128, n, KD * 128,
                                          [(kind, half * 11 + i + k, k * KD * 128)
                                           for k in range(n)]))
                    for i in range(16):
                        off = (MW_OFF["sd"] + i * 128 * NDS * 128 + half * NDE * 128)
                        units.append((off, NDE * 128, 1, NDS * 128,
                                      [("d", half * 16 + i, 0)]))
            return units

        def issue_units(sy, units, events):
            for (off, blen, nblk, pstride, _blks), ev in zip(units, events):
                idx = ring["n"]; ring["n"] += 1
                if idx >= NSL:
                    sem_nm, cnt = unit_free[idx - NSL]
                    cx.w(sy, sem_nm, cnt)
                unit_free.append(ev)
                sl = idx % NSL
                if nblk == 1:
                    srcp = _ap(mw, off, [[pstride, 128], [1, blen]])
                else:
                    srcp = _ap(mw, off, [[blen, 128], [128 * blen, nblk], [1, blen]])
                cx.inc(sy.dma_start(out=moe_w[:, sl * SLW:sl * SLW + nblk * blen],
                                    in_=srcp), "wmd", 16)

        def events_for(e, pg0, pu0, pd0):
            ev = []
            if e < E:
                for i in range(0, NDE, 2):
                    f = min(i + 2, NDE)
                    ev.append(("pg", pg0 + f))
                    ev.append(("pu", pu0 + f))
                for i in range(0, 16, 2):
                    ev.append(("pd", pd0 + i + 2))
            else:
                for half in range(2):
                    for i in range(0, 11, 2):
                        f = min(i + 2, 11)
                        ev.append(("pg", pg0 + half * 11 + f))
                        ev.append(("pu", pu0 + half * 11 + f))
                    for i in range(16):
                        ev.append(("pd", pd0 + half * 16 + i + 1))
            return ev

        def unit_map(units, start_idx):
            m = {}
            for k, (off, blen, nblk, pstride, blks) in enumerate(units):
                idx = start_idx + k
                for kind, mi, coff in blks:
                    m[(kind, mi)] = (idx, (idx % NSL) * SLW + coff)
            return m

        for c in range(NCH):
            bA = {k: cx.val(k) for k in
                  ("dx", "lns", "tp", "xe", "rt", "vp", "vse", "tvp", "vte",
                   "sop", "sfe", "csm", "cse", "ctp", "cte", "cdd", "cbp",
                   "cbe", "ox1")}

            with nc.Block() as blk:
                @blk.sync
                def _(sy):
                    cx.w(sy, "ox1", bA["ox1"])
                    for tsub in range(TS):
                        if tsub >= 2:
                            cx.w(sy, "lns", bA["lns"] + tsub - 1)
                        cx.inc(sy.dma_start(
                            out=x_tile[:, (tsub % 2) * D:(tsub % 2) * D + D],
                            in_=_ap(T["x"], (c * CH + tsub * 128) * D,
                                    [[D, 128], [1, D]])), "dx", 16)

                @blk.gpsimd
                def _(g):
                    cx.w(g, "cte", bA["cte"] + 1)
                    cx.w(g, "cdd", bA["cdd"])
                    cx.inc(g.dma_start(out=T["g_tmp"][0:8, :], in_=comb_T[:, :]),
                           "cdd", 16)
                    cx.w(g, "cdd", bA["cdd"] + 16)
                    cx.inc(g.dma_start(
                        out=comb_rows[0:1, 0:E * CH],
                        in_=_ap(T["g_tmp"], 0, [[E * CH, 1], [1, E * CH]])), "cdd", 16)

                @blk.scalar
                def _(s):
                    if c == 0:
                        cx.w(s, "ldm", cx.val("ldm"))
                        cx.w(s, "ldh", cx.val("ldh"))
                    for tsub in range(TS):
                        cx.w(s, "dx", bA["dx"] + 16 * (tsub + 1))
                        if tsub >= 2:
                            cx.w(s, "tp", bA["tp"] + (tsub - 1) * KD)
                        ln_scalar(s, x_tile[:, (tsub % 2) * D:(tsub % 2) * D + D],
                                  xn_tm[:, (tsub % 2) * D:(tsub % 2) * D + D],
                                  s_fm[:, 0:D], "lns")
                    for tsub in range(TS):
                        cx.w(s, "vp", bA["vp"] + tsub + 1)
                        if tsub >= 2:
                            cx.w(s, "tvp", bA["tvp"] + tsub - 1)
                        ins = s.activation(
                            vs_t[:, (tsub % 2) * 128:(tsub % 2) * 128 + 128],
                            p_a[tsub % 2][:, 0:128], AF.Copy)
                        cx.inc(ins, "vse", 1)
                        cx.w(s, "csm", bA["csm"] + tsub * 3 + 1)
                        ins = s.activation(cstat[:, 8:16], cstat[:, 0:8], AF.Exp,
                                           bias=cstat[:, 16:17])
                        cx.inc(ins, "cse", 1)
                        for mi in range(16):
                            n = tsub * 16 + mi
                            cx.w(s, "sop", bA["sop"] + n + 1)
                            ins = s.activation(
                                s_fm[:, mi * 512 + tsub * 128:
                                     mi * 512 + tsub * 128 + 128],
                                p_a[2 + (n % 2)][:, 0:128], AF.Identity,
                                bias=c_pt[:, mi:mi + 1])
                            cx.inc(ins, "sfe", 1)

                @blk.tensor
                def _(t):
                    if c == 0:
                        cx.w(t, "ldh", cx.val("ldh"))
                        cx.w(t, "ldm", cx.val("ldm"))
                        cx.w(t, "id_v", cx.val("id_v"))
                    for tsub in range(TS):
                        cx.w(t, "lns", bA["lns"] + tsub + 1)
                        for ki in range(KD):
                            n = tsub * KD + ki
                            if n >= 2:
                                cx.w(t, "xe", bA["xe"] + n - 1)
                            ins = t.transpose(
                                p_t[n % 2][:, :],
                                xn_tm[:, (tsub % 2) * D + ki * 128:
                                      (tsub % 2) * D + ki * 128 + 128], ident[:, :])
                            cx.inc(ins, "tp", 1)
                    for tsub in range(TS):
                        cx.w(t, "xe", bA["xe"] + (tsub + 1) * KD)
                        if tsub >= 2:
                            cx.w(t, "vse", bA["vse"] + tsub - 1)
                        for ki in range(KD):
                            ins = t.matmul(
                                p_a[tsub % 2][:, 0:128],
                                xn_fm[:, ki * 512 + tsub * 128:
                                      ki * 512 + tsub * 128 + 128],
                                wv_sb[:, ki * 128:ki * 128 + 128],
                                start=(ki == 0), stop=(ki == KD - 1))
                        cx.inc(ins, "vp", 1)
                        if tsub >= 1:
                            cx.w(t, "csm", bA["csm"] + (tsub - 1) * 3 + 1)
                        for ki in range(KD):
                            ins = t.matmul(p_r8[:, :],
                                           xn_fm[:, ki * 512 + tsub * 128:
                                                 ki * 512 + tsub * 128 + 128],
                                           router_w[:, ki * 8:ki * 8 + 8],
                                           start=(ki == 0), stop=(ki == KD - 1))
                        cx.inc(ins, "rt", 1)
                        cx.w(t, "vse", bA["vse"] + tsub + 1)
                        if tsub < 2:
                            cx.w(t, "xe", bA["xe"] + TS * KD)
                        else:
                            cx.w(t, "vte", bA["vte"] + tsub - 1)
                        ins = t.transpose(p_t[tsub % 2][:, :],
                                          vs_t[:, (tsub % 2) * 128:
                                               (tsub % 2) * 128 + 128], ident[:, :])
                        cx.inc(ins, "tvp", 1)
                        cx.w(t, "vte", bA["vte"] + tsub + 1)
                        for mi in range(16):
                            n = tsub * 16 + mi
                            if n >= 2:
                                cx.w(t, "sfe", bA["sfe"] + n - 1)
                            ins = t.matmul(
                                p_a[2 + (n % 2)][:, 0:128],
                                ws_sb[:, mi * 128:mi * 128 + 128],
                                vsT[:, (tsub % 2) * 128:(tsub % 2) * 128 + 128],
                                start=True, stop=True)
                            cx.inc(ins, "sop", 1)
                    cx.w(t, "csm", bA["csm"] + TS * 3)
                    for tsub in range(TS):
                        ins = t.transpose(p_c8[:, tsub * 128:tsub * 128 + 128],
                                          comb_tm[:, tsub * 8:tsub * 8 + 8],
                                          identf[:, :])
                        cx.inc(ins, "ctp", 1)
                    cx.w(t, "cdd", bA["cdd"] + 32)
                    cx.w(t, "vse", bA["vse"] + TS)
                    for e in range(E):
                        if e >= 2:
                            cx.w(t, "cbe", bA["cbe"] + e - 1)
                        ins = t.matmul(p_a[e % 2][:, :], ones_r[0:1, 0:128],
                                       comb_rows[0:1, e * 512:e * 512 + 512],
                                       start=True, stop=True)
                        cx.inc(ins, "cbp", 1)

                @blk.vector
                def _(v):
                    for tsub in range(TS):
                        for ki in range(KD):
                            n = tsub * KD + ki
                            cx.w(v, "tp", bA["tp"] + n + 1)
                            ins = v.tensor_copy(
                                xn_fm[:, ki * 512 + tsub * 128:
                                      ki * 512 + tsub * 128 + 128],
                                p_t[n % 2][:, :])
                            cx.inc(ins, "xe", 1)
                    for tsub in range(TS):
                        cx.w(v, "rt", bA["rt"] + tsub + 1)
                        ins = v.tensor_copy(cstat[:, 0:8], p_r8[:, :])
                        cx.fence(v, ins, "fv")
                        ins = v.reduce_max(cstat[:, 16:17], cstat[:, 0:8], AX.X,
                                           negate=True)
                        cx.inc(ins, "csm", 1)
                        cx.w(v, "cse", bA["cse"] + tsub + 1)
                        ins = v.reduce_sum(cstat[:, 17:18], cstat[:, 8:16], AX.X)
                        cx.fence(v, ins, "fv")
                        ins = v.reciprocal(cstat[:, 18:19], cstat[:, 17:18])
                        cx.fence(v, ins, "fv")
                        ins = v.tensor_scalar(cstat[:, 0:8], cstat[:, 8:16],
                                              cstat[:, 18:19], None, ALU.mult)
                        cx.fence(v, ins, "fv")
                        ins = v.reduce_max(cstat[:, 19:20], cstat[:, 0:8], AX.X)
                        cx.fence(v, ins, "fv")
                        ins = v.tensor_scalar(cstat[:, 8:16], cstat[:, 0:8],
                                              cstat[:, 19:20], None, ALU.is_ge)
                        cx.fence(v, ins, "fv")
                        ins = v.tensor_sub(cstat[:, 8:16], cstat[:, 0:8],
                                           cstat[:, 8:16])
                        cx.fence(v, ins, "fv")
                        ins = v.reduce_max(cstat[:, 20:21], cstat[:, 8:16], AX.X)
                        cx.fence(v, ins, "fv")
                        ins = v.tensor_scalar(cstat[:, 8:16], cstat[:, 0:8],
                                              cstat[:, 20:21], None, ALU.is_ge)
                        cx.fence(v, ins, "fv")
                        ins = v.tensor_mul(comb_tm[:, tsub * 8:tsub * 8 + 8],
                                           cstat[:, 0:8], cstat[:, 8:16])
                        cx.inc(ins, "csm", 2)
                        cx.w(v, "tvp", bA["tvp"] + tsub + 1)
                        if tsub >= 2:
                            cx.w(v, "sop", bA["sop"] + (tsub - 1) * 16)
                        ins = v.tensor_copy(vsT[:, (tsub % 2) * 128:
                                                (tsub % 2) * 128 + 128],
                                            p_t[tsub % 2][:, :])
                        cx.inc(ins, "vte", 1)
                    cx.w(v, "ctp", bA["ctp"] + TS)
                    ins = v.tensor_copy(comb_T[:, :], p_c8[0:8, 0:512])
                    cx.inc(ins, "cte", 1)
                    for e in range(E):
                        cx.w(v, "cbp", bA["cbp"] + e + 1)
                        ins = v.tensor_copy(comb_bc[:, e * 512:e * 512 + 512],
                                            p_a[e % 2][:, :])
                        cx.inc(ins, "cbe", 1)

            with nc.Block() as bblk:
                for e in range(E + 1):
                    _p1_stage_B(nc, cx, bblk, c, e, xn_fm, comb_bc, h_buf,
                                racc, silu_t, h_t, p_a, moe_w, NSL, SLW,
                                units_for, unit_map, issue_units, events_for,
                                ring)

            _p1_stage_CE(nc, cx, T, c, ident, racc, g_fm, s_fm, x_tile, xn_tm,
                         sg_tm, xn2f_t, h_buf, lnst, p_t, ln_scalar)

        with nc.Block() as blk:
            @blk.sync
            def _(sy):
                cx.wall(sy, "ox1")
                cx.wall(sy, "oxn2")
        if own_cx:
            _clear_sems(nc, cx)


def _p1_stage_B(nc, cx, blk, c, e, xn_fm, comb_bc, h_buf, racc, silu_t, h_t,
                p_a, moe_w, NSL, SLW, units_for, unit_map, issue_units,
                events_for, ring):
    units = units_for(e)
    m = unit_map(units, ring["n"])
    shared = (e == E)
    ng = 11 if shared else NDE
    halves = 2 if shared else 1

    bB = {k: cx.val(k) for k in ("pg", "pu", "psil", "hv1", "hv", "pd", "rv",
                                 "wmd")}

    if True:
        @blk.sync
        def _(sy):
            issue_units(sy, units, events_for(e, bB["pg"], bB["pu"], bB["pd"]))

        @blk.tensor
        def _(t):
            if e > 0:
                cx.w(t, "rv", bB["rv"])
                cx.w(t, "hv", bB["hv"])
            for half in range(halves):
                for i in range(ng):
                    key = half * 11 + i
                    uidx_g, coff_g = m[("g", key)]
                    uidx_u, coff_u = m[("u", key)]
                    cx.w(t, "wmd", 16 * (max(uidx_g, uidx_u) + 1))
                    nh = half * ng + i
                    if nh >= 2:
                        cx.w(t, "hv", bB["hv"] + nh - 1)
                    sg_ = 2 * (nh % 2)
                    for ki in range(KD):
                        ins = t.matmul(
                            p_a[sg_][:, :],
                            moe_w[:, coff_g + ki * 128:coff_g + ki * 128 + 128],
                            xn_fm[:, ki * 512:ki * 512 + 512],
                            start=(ki == 0), stop=(ki == KD - 1))
                    cx.inc(ins, "pg", 1)
                    for ki in range(KD):
                        ins = t.matmul(
                            p_a[sg_ + 1][:, :],
                            moe_w[:, coff_u + ki * 128:coff_u + ki * 128 + 128],
                            xn_fm[:, ki * 512:ki * 512 + 512],
                            start=(ki == 0), stop=(ki == KD - 1))
                    cx.inc(ins, "pu", 1)
                for mi in range(16):
                    key = half * 16 + mi
                    uidx_d, coff_d = m[("d", key)]
                    cx.w(t, "wmd", 16 * (uidx_d + 1))
                    nd = half * 16 + mi
                    if mi == 0:
                        cx.w(t, "hv", bB["hv"] + (half + 1) * ng)
                    ndg = half * 16 + mi
                    if ndg >= 2:
                        cx.w(t, "rv", bB["rv"] + ndg - 1)
                    for ki in range(NDE):
                        ins = t.matmul(
                            p_a[ndg % 2][:, :],
                            moe_w[:, coff_d + ki * 128:coff_d + ki * 128 + 128],
                            h_buf[:, ki * 512:ki * 512 + 512],
                            start=(ki == 0), stop=(ki == NDE - 1))
                    cx.inc(ins, "pd", 1)

        @blk.scalar
        def _(s):
            for half in range(halves):
                for i in range(ng):
                    nh = half * ng + i
                    cx.w(s, "pg", bB["pg"] + nh + 1)
                    if nh >= 2:
                        cx.w(s, "hv" if shared else "hv1", bB["hv" if shared
                             else "hv1"] + nh - 1)
                    ins = s.activation(silu_t[:, (nh % 2) * CH:(nh % 2) * CH + CH],
                                       p_a[2 * (nh % 2)][:, :], AF.Silu)
                    cx.inc(ins, "psil", 1)

        @blk.gpsimd
        def _(g):
            if not shared:
                for i in range(ng):
                    cx.w(g, "hv1", bB["hv1"] + i + 1)
                    ins = g.tensor_mul(h_buf[:, i * 512:i * 512 + 512],
                                       h_t[:, (i % 2) * CH:(i % 2) * CH + CH],
                                       comb_bc[:, e * 512:e * 512 + 512])
                    cx.inc(ins, "hv", 1)

        @blk.vector
        def _(v):
            for half in range(halves):
                for i in range(ng):
                    nh = half * ng + i
                    cx.w(v, "psil", bB["psil"] + nh + 1)
                    cx.w(v, "pu", bB["pu"] + nh + 1)
                    sl = (nh % 2)
                    if shared:
                        ins = v.scalar_tensor_tensor(
                            h_buf[:, i * 512:i * 512 + 512],
                            p_a[2 * sl + 1][:, :], 1.0,
                            silu_t[:, sl * CH:sl * CH + CH], ALU.mult, ALU.mult)
                        cx.inc(ins, "hv", 1)
                    else:
                        if nh >= 2:
                            cx.w(v, "hv", bB["hv"] + nh - 1)
                        ins = v.scalar_tensor_tensor(
                            h_t[:, sl * CH:sl * CH + CH],
                            p_a[2 * sl + 1][:, :], 1.0,
                            silu_t[:, sl * CH:sl * CH + CH], ALU.mult, ALU.mult)
                        cx.inc(ins, "hv1", 1)
                for mi in range(16):
                    nd = half * 16 + mi
                    cx.w(v, "pd", bB["pd"] + nd + 1)
                    if e == 0:
                        ins = v.tensor_scalar(racc[:, mi * 512:mi * 512 + 512],
                                              p_a[nd % 2][:, :], 1.0, None, ALU.mult)
                    else:
                        ins = v.scalar_tensor_tensor(
                            racc[:, mi * 512:mi * 512 + 512],
                            p_a[nd % 2][:, :], 1.0,
                            racc[:, mi * 512:mi * 512 + 512], ALU.mult, ALU.add)
                    cx.inc(ins, "rv", 1)


def _p1_stage_CE(nc, cx, T, c, ident, racc, g_fm, s_fm, x_tile, xn_tm, sg_tm,
                 xn2f_t, ep_junk, lnst, p_t, ln_scalar):
    bC = {k: cx.val(k) for k in ("gs", "sgv", "etp", "ete", "dx2", "x1v",
                                 "lns2", "ox1", "oxn2")}

    with nc.Block() as blk:
        @blk.scalar
        def _(s):
            for mi in range(16):
                ins = s.activation(g_fm[:, mi * 512:mi * 512 + 512],
                                   racc[:, mi * 512:mi * 512 + 512], AF.Silu)
                cx.inc(ins, "gs", 1)
            for tsub in range(TS):
                cx.w(s, "x1v", bC["x1v"] + tsub + 1)
                if tsub >= 2:
                    cx.w(s, "etp", bC["etp"] + tsub * 32 - 16)
                ln_scalar(s, x_tile[:, (tsub % 2) * D:(tsub % 2) * D + D],
                          xn_tm[:, (tsub % 2) * D:(tsub % 2) * D + D],
                          ep_junk[:, 0:D], "lns2")

        @blk.vector
        def _(v):
            for mi in range(16):
                cx.w(v, "gs", bC["gs"] + mi + 1)
                ins = v.tensor_mul(s_fm[:, mi * 512:mi * 512 + 512],
                                   s_fm[:, mi * 512:mi * 512 + 512],
                                   g_fm[:, mi * 512:mi * 512 + 512])
                cx.inc(ins, "sgv", 1)
            for tsub in range(TS):
                for mi in range(16):
                    n = tsub * 32 + mi
                    cx.w(v, "etp", bC["etp"] + n + 1)
                    ins = v.tensor_copy(sg_tm[:, mi * 128:mi * 128 + 128],
                                        p_t[n % 2][:, :])
                    cx.inc(ins, "ete", 1)
                cx.wall(v, "ete")
                cx.w(v, "dx2", bC["dx2"] + 16 * (tsub + 1))
                if tsub >= 2:
                    cx.w(v, "ox1", bC["ox1"] + 16 * (tsub - 1))
                ins = v.tensor_add(x_tile[:, (tsub % 2) * D:(tsub % 2) * D + D],
                                   x_tile[:, (tsub % 2) * D:(tsub % 2) * D + D],
                                   sg_tm[:, :])
                cx.inc(ins, "x1v", 1)
                if tsub >= 1:
                    cx.w(v, "oxn2", bC["oxn2"] + 16 * tsub)
                for mi in range(16):
                    n = tsub * 32 + 16 + mi
                    cx.w(v, "etp", bC["etp"] + n + 1)
                    ins = v.tensor_copy(xn2f_t[:, mi * 128:mi * 128 + 128],
                                        p_t[n % 2][:, :])
                    cx.inc(ins, "ete", 1)

        @blk.tensor
        def _(t):
            for tsub in range(TS):
                for mi in range(16):
                    n = tsub * 32 + mi
                    cx.w(t, "sgv", bC["sgv"] + mi + 1)
                    if n >= 2:
                        cx.w(t, "ete", bC["ete"] + n - 1)
                    ins = t.transpose(
                        p_t[n % 2][:, :],
                        s_fm[:, mi * 512 + tsub * 128:mi * 512 + tsub * 128 + 128],
                        ident[:, :])
                    cx.inc(ins, "etp", 1)
                cx.w(t, "lns2", bC["lns2"] + tsub + 1)
                for mi in range(16):
                    n = tsub * 32 + 16 + mi
                    if n >= 2:
                        cx.w(t, "ete", bC["ete"] + n - 1)
                    ins = t.transpose(
                        p_t[n % 2][:, :],
                        xn_tm[:, (tsub % 2) * D + mi * 128:
                              (tsub % 2) * D + mi * 128 + 128], ident[:, :])
                    cx.inc(ins, "etp", 1)

        @blk.sync
        def _(sy):
            for tsub in range(TS + 2):
                if tsub >= 2:
                    ts2 = tsub - 2
                    cx.w(sy, "x1v", bC["x1v"] + ts2 + 1)
                    cx.inc(sy.dma_start(
                        out=_ap(T["g_x1"], (c * CH + ts2 * 128) * D,
                                [[D, 128], [1, D]]),
                        in_=x_tile[:, (ts2 % 2) * D:(ts2 % 2) * D + D]), "ox1", 16)
                    cx.w(sy, "ete", bC["ete"] + ts2 * 32 + 32)
                    cx.inc(sy.dma_start(
                        out=_ap(T["g_xn2"], c * CH + ts2 * 128,
                                [[S, 128], [128 * S, KD], [1, 128]]),
                        in_=xn2f_t[:, :].rearrange("p (k m) -> p k m", m=128)),
                        "oxn2", 16)
                if tsub < TS:
                    if tsub >= 2:
                        cx.w(sy, "lns2", bC["lns2"] + tsub - 1)
                        cx.w(sy, "ox1", bC["ox1"] + 16 * (tsub - 1))
                    cx.inc(sy.dma_start(
                        out=x_tile[:, (tsub % 2) * D:(tsub % 2) * D + D],
                        in_=_ap(T["x"], (c * CH + tsub * 128) * D,
                                [[D, 128], [1, D]])), "dx2", 16)


def emit_p2(nc, T, cx=None):
    with ExitStack() as st:
        own_cx = cx is None
        cx = cx if cx is not None else Ctx(nc, st)
        sb = lambda nm, shp, dt: st.enter_context(nc.sbuf_tensor(nm, shp, dt))

        ident = sb("p2_id", [128, 128], BF16)
        iota_s = sb("p2_io", [128, 128], F32)
        ones_r = sb("p2_on", [1, 512], BF16)
        xn2s = sb("p2_xn", [128, KD * S], BF16)
        wq_w = sb("p2_w", [128, 2 * KD * 128], BF16)
        qk_sb = sb("p2_qk", [128, 2 * S], BF16)
        v_sb = sb("p2_v", [128, 2 * 32 * 128], BF16)
        b1_sb = sb("p2_b1", [48, 128], BF16)
        b1_pt = sb("p2_b1p", [128, 48], BF16)
        bv_row = sb("p2_bv", [1, D], BF16)
        cos_sb = sb("p2_cos", [32, S], BF16)
        sin_sb = sb("p2_sin", [32, S], BF16)
        pmat_sb = sb("p2_pm", [32, 32], BF16)
        rt1 = sb("p2_rt1", [32, 512], F32)
        rt2 = sb("p2_rt2", [32, 512], F32)

        p_f = [st.enter_context(nc.psum_tensor(f"p2_pf{i}", [128, 512], F32))
               for i in range(2)]
        p_v = [st.enter_context(nc.psum_tensor(f"p2_pv{i}", [128, 128], F32))
               for i in range(2)]
        p_r = [st.enter_context(nc.psum_tensor(f"p2_pr{i}", [32, 512], F32))
               for i in range(2)]
        p_bt = st.enter_context(nc.psum_tensor("p2_pbt", [128, 48], BF16))

        _emit_ident(nc, cx, ident, iota_s)

        with nc.Block() as blk:
            @blk.sync
            def _(sy):
                for ki in range(KD):
                    cx.inc(sy.dma_start(
                        out=xn2s[:, ki * S:ki * S + S],
                        in_=_ap(T["g_xn2"], ki * 128 * S, [[S, 128], [1, S]])),
                        "xnl", 16)


            @blk.gpsimd
            def _(g):
                g.memset(ones_r[:, :], 1.0)
                cx.inc(g.dma_start(out=cos_sb[:, :], in_=T["cosb"][:, :]), "cl", 16)
                cx.inc(g.dma_start(out=sin_sb[:, :], in_=T["sinb"][:, :]), "cl", 16)
                cx.inc(g.dma_start(out=b1_sb[:, :],
                                   in_=_ap(T["b1"], 0, [[128, 48], [1, 128]])),
                       "cl", 16)
                cx.inc(g.dma_start(out=bv_row[0:1, :],
                                   in_=_ap(T["b1"], 256, [[1, 1], [384, 16], [1, 128]])),
                       "cl", 16)
                cx.inc(g.dma_start(out=pmat_sb[:, :], in_=T["pmat"][:, :]), "cl", 16)

        with nc.Block() as blk:
            @blk.tensor
            def _(t):
                cx.w(t, "cl", cx.val("cl"))
                cx.w(t, "id_v", cx.val("id_v"))
                ins = t.transpose(p_bt[:, 0:48], b1_sb[:, :], ident[0:48, 0:48])
                cx.inc(ins, "bt", 1)

            @blk.vector
            def _(v):
                cx.w(v, "bt", 1)
                cx.inc(v.tensor_copy(b1_pt[:, :], p_bt[:, 0:48]), "btv", 1)

        wfree = []
        nqk = [0]
        nv = [0]

        with nc.Block() as blk:
            @blk.gpsimd
            def _(g):
                for mi in range(48):
                    part = mi % 3
                    if mi >= 2:
                        cx.w(g, *wfree[mi - 2])
                    cx.inc(g.dma_start(
                        out=wq_w[:, (mi % 2) * 2048:(mi % 2) * 2048 + 2048],
                        in_=_ap(T["wqkv1"], mi * 128,
                                [[3 * D, 128], [128 * 3 * D, KD], [1, 128]])),
                        "wql", 16)
                    if part < 2:
                        k = len([m for m in range(mi + 1) if m % 3 < 2])
                        wfree.append(("qe2", 8 * k))
                    else:
                        k = len([m for m in range(mi + 1) if m % 3 == 2])
                        wfree.append(("ve2", 32 * k))

            @blk.tensor
            def _(t):
                cx.w(t, "xnl", 16 * KD)
                cx.w(t, "cl", cx.val("cl"))
                cx.w(t, "btv", 1)
                gq = 0
                gv = 0
                gr = 0
                for mi in range(48):
                    part, h = mi % 3, mi // 3
                    sl = (mi % 2) * 2048
                    cx.w(t, "wql", 16 * (mi + 1))
                    if part < 2:
                        for tc in range(8):
                            if gq >= 2:
                                cx.w(t, "qe2", gq - 1)
                            for ki in range(KD):
                                ins = t.matmul(
                                    p_f[gq % 2][:, :],
                                    wq_w[:, sl + ki * 128:sl + ki * 128 + 128],
                                    xn2s[:, ki * S + tc * 512:ki * S + tc * 512 + 512],
                                    start=(ki == 0), stop=(ki == KD - 1))
                            cx.inc(ins, "qp2", 1)
                            gq += 1
                        for tc in range(8):
                            cx.w(t, "qe2", gq - 8 + tc + 1)
                            if gr >= 2:
                                cx.w(t, "rv2", gr - 1)
                            qsl = ((gq - 8) // 8 % 2) * S
                            ins = t.matmul(
                                p_r[gr % 2][:, :],
                                pmat_sb[:, :],
                                qk_sb[0:32, qsl + tc * 512:qsl + tc * 512 + 512],
                                start=True, stop=True)
                            cx.inc(ins, "rp2", 1)
                            gr += 1
                    else:
                        for tsub in range(32):
                            if gv >= 2:
                                cx.w(t, "ve2", gv - 1)
                            for ki in range(KD):
                                t.matmul(
                                    p_v[gv % 2][:, :],
                                    xn2s[:, ki * S + tsub * 128:ki * S + tsub * 128 + 128],
                                    wq_w[:, sl + ki * 128:sl + ki * 128 + 128],
                                    start=(ki == 0), stop=False)
                            ins = t.matmul(p_v[gv % 2][:, :], ones_r[0:1, 0:128],
                                           bv_row[0:1, h * 128:h * 128 + 128],
                                           start=False, stop=True)
                            cx.inc(ins, "vp2", 1)
                            gv += 1

            @blk.scalar
            def _(s):
                gq = 0
                for mi in range(48):
                    part, h = mi % 3, mi // 3
                    if part == 2:
                        continue
                    nq = gq // 8
                    qsl = (nq % 2) * S
                    for tc in range(8):
                        cx.w(s, "qp2", gq + 1)
                        if nq >= 2 and tc == 0:
                            cx.w(s, "oqk", 16 * (nq - 1))
                        ins = s.activation(qk_sb[:, qsl + tc * 512:qsl + tc * 512 + 512],
                                           p_f[gq % 2][:, :], AF.Identity,
                                           bias=b1_pt[:, mi:mi + 1])
                        cx.inc(ins, "qe2", 1)
                        gq += 1

            @blk.vector
            def _(v):
                gq = gr = gv = 0
                for mi in range(48):
                    part, h = mi % 3, mi // 3
                    if part < 2:
                        nq = gq // 8
                        qsl = (nq % 2) * S
                        gq += 8
                        for tc in range(8):
                            cx.w(v, "rp2", gr + 1)
                            ins = v.tensor_mul(rt1[:, :], p_r[gr % 2][:, :],
                                               sin_sb[:, tc * 512:tc * 512 + 512])
                            ins = v.tensor_mul(
                                rt2[:, :],
                                qk_sb[0:32, qsl + tc * 512:qsl + tc * 512 + 512],
                                cos_sb[:, tc * 512:tc * 512 + 512])
                            cx.fence(v, ins, "fv")
                            ins = v.tensor_add(
                                qk_sb[0:32, qsl + tc * 512:qsl + tc * 512 + 512],
                                rt1[:, :], rt2[:, :])
                            cx.inc(ins, "rv2", 1)
                            gr += 1
                    else:
                        nvv = gv // 32
                        vsl = (nvv % 2) * S
                        for tsub in range(32):
                            cx.w(v, "vp2", gv + 1)
                            if nvv >= 2 and tsub == 0:
                                cx.w(v, "ovt", 16 * (nvv - 1))
                            ins = v.tensor_copy(
                                v_sb[:, vsl + tsub * 128:vsl + tsub * 128 + 128],
                                p_v[gv % 2][:, :])
                            cx.inc(ins, "ve2", 1)
                            gv += 1

            @blk.sync
            def _(sy):
                gq = gv = 0
                for mi in range(48):
                    part, h = mi % 3, mi // 3
                    if part < 2:
                        nq = gq // 8
                        qsl = (nq % 2) * S
                        gq += 8
                        cx.w(sy, "rv2", 8 * (nq + 1))
                        cx.inc(sy.dma_start(
                            out=_ap(T["g_qk"], (part * NH + h) * 128 * S,
                                    [[S, 128], [1, S]]),
                            in_=qk_sb[:, qsl:qsl + S]), "oqk", 16)
                    else:
                        nvv = gv // 32
                        vsl = (nvv % 2) * S
                        gv += 32
                        cx.w(sy, "ve2", 32 * (nvv + 1))
                        cx.inc(sy.dma_start(
                            out=_ap(T["g_vtm"], h * 128,
                                    [[D, 128], [128 * D, 32], [1, 128]]),
                            in_=v_sb[:, vsl:vsl + S].rearrange(
                                "p (k m) -> p k m", m=128)), "ovt", 16)

        with nc.Block() as blk:
            @blk.sync
            def _(sy):
                cx.wall(sy, "oqk")
                cx.wall(sy, "ovt")
        if own_cx:
            _clear_sems(nc, cx)


def emit_p3(nc, T, cx=None):
    with ExitStack() as st:
        own_cx = cx is None
        cx = cx if cx is not None else Ctx(nc, st)
        sb = lambda nm, shp, dt: st.enter_context(nc.sbuf_tensor(nm, shp, dt))

        ident = sb("p3_id", [128, 128], BF16)
        iota_s = sb("p3_io", [128, 128], F32)
        ones_c = sb("p3_onc", [128, 1], BF16)
        ones_rb = sb("p3_onb", [1, 128], BF16)
        attn_sb = sb("p3_at", [128, NH * S], BF16)
        mask_sb = sb("p3_mk", [128, 4 * 512], F32)
        bo_sb = sb("p3_bo", [16, 128], BF16)
        bo_pt = sb("p3_bop", [128, 16], BF16)

        _emit_ident(nc, cx, ident, iota_s)

        with nc.Block() as blk:
            @blk.sync
            def _(sy):
                cx.inc(sy.dma_start(out=mask_sb[:, :], in_=T["mask4"][:, :]), "cl3", 16)

            @blk.gpsimd
            def _(g):
                g.memset(ones_c[:, :], 1.0)
                g.memset(ones_rb[:, :], 1.0)
                cx.inc(g.dma_start(out=bo_sb[:, :],
                                   in_=_ap(T["bo1"], 0, [[128, 16], [1, 128]])),
                       "cl3g", 16)

        with ExitStack() as st2:
            sb2 = lambda nm, shp, dt: st2.enter_context(nc.sbuf_tensor(nm, shp, dt))
            q_sb = sb2("p3_q", [128, 2 * S], BF16)
            k_sb = sb2("p3_k", [128, 2 * S], BF16)
            v_sb = sb2("p3_v", [128, 2 * S], BF16)
            expT = sb2("p3_e", [128, 2 * 512], BF16)
            msk_s = sb2("p3_ms", [128, 2 * 512], F32)
            rc_sb = sb2("p3_rc", [1, 2 * 512], BF16)
            rcf32 = sb2("p3_rcf", [1, 512], F32)
            rb_sb = sb2("p3_rb", [128, 2 * 512], BF16)
            p_s = [st2.enter_context(nc.psum_tensor(f"p3_ps{i}", [128, 512], F32))
                   for i in range(2)]
            p_o = [st2.enter_context(nc.psum_tensor(f"p3_po{i}", [128, 512], F32))
                   for i in range(2)]
            p_d = [st2.enter_context(nc.psum_tensor(f"p3_pd{i}", [1, 512], F32))
                   for i in range(2)]
            p_b = st2.enter_context(nc.psum_tensor("p3_pb", [128, 512], F32))

            def kts(qc):
                return 4 * qc + 4

            with nc.Block() as blk:
                @blk.sync
                def _(sy):
                    for h in range(NH):
                        if h >= 2:
                            cx.w(sy, "aat", 8 * (h - 1))
                        hsl = (h % 2) * S
                        cx.inc(sy.dma_start(
                            out=q_sb[:, hsl:hsl + S],
                            in_=_ap(T["g_qk"], h * 128 * S, [[S, 128], [1, S]])),
                            "qkl", 16)
                        cx.inc(sy.dma_start(
                            out=k_sb[:, hsl:hsl + S],
                            in_=_ap(T["g_qk"], (NH + h) * 128 * S,
                                    [[S, 128], [1, S]])), "qkl", 16)
                        cx.inc(sy.dma_start(
                            out=v_sb[:, hsl:hsl + S],
                            in_=_ap(T["g_vtm"], h * 128,
                                    [[D, 128], [128 * D, 32], [1, 128]])),
                            "qkl", 16)

                @blk.tensor
                def _(t):
                    cx.w(t, "cl3", 16)
                    npst = 0
                    for h in range(NH):
                        hsl = (h % 2) * S
                        cx.w(t, "qkl", 48 * (h + 1))
                        for qc in range(8):
                            aq = h * 8 + qc
                            nkt = kts(qc)
                            base_pst = npst
                            for kt in range(nkt):
                                if npst >= 2:
                                    cx.w(t, "xps", npst - 1)
                                ins = t.matmul(
                                    p_s[kt % 2][:, :],
                                    k_sb[:, hsl + kt * 128:hsl + kt * 128 + 128],
                                    q_sb[:, hsl + qc * 512:hsl + qc * 512 + 512],
                                    start=True, stop=True)
                                cx.inc(ins, "pst", 1)
                                npst += 1
                                ktp = kt - 1
                                if ktp >= 0:
                                    _p3_pv_den(cx, t, h, qc, ktp, nkt, aq, hsl,
                                               base_pst, v_sb, expT, p_o, p_d,
                                               ones_c)
                            _p3_pv_den(cx, t, h, qc, nkt - 1, nkt, aq, hsl,
                                       base_pst, v_sb, expT, p_o, p_d, ones_c)
                            cx.w(t, "rcv", aq + 1)
                            if aq >= 1:
                                cx.w(t, "rbe", aq)
                            ins = t.matmul(p_b[:, :], ones_rb[0:1, :],
                                           rc_sb[0:1, (aq % 2) * 512:(aq % 2) * 512 + 512],
                                           start=True, stop=True)
                            cx.inc(ins, "bcp", 1)

                @blk.scalar
                def _(s):
                    npst = 0
                    for h in range(NH):
                        for qc in range(8):
                            aq = h * 8 + qc
                            nkt = kts(qc)
                            for kt in range(nkt):
                                cx.w(s, "pst", npst + 1)
                                if npst >= 2:
                                    cx.w(s, "pvp", npst - 1)
                                if kt >= 4 * qc:
                                    cx.w(s, "mskv", cx.val("mskv") - 0) \
                                        if False else None
                                    cx.w(s, "mskv", _p3_mskcnt(h, qc, kt) + 1)
                                    src_ap = msk_s[:, (kt % 2) * 512:(kt % 2) * 512 + 512]
                                else:
                                    src_ap = p_s[kt % 2][:, :]
                                ins = s.activation(
                                    expT[:, (kt % 2) * 512:(kt % 2) * 512 + 512],
                                    src_ap, AF.Exp)
                                cx.inc(ins, "xps", 1)
                                npst += 1
                            cx.w(s, "bcp", aq + 1)
                            if aq >= 1:
                                cx.w(s, "aat", aq)
                            ins = s.activation(
                                rb_sb[:, (aq % 2) * 512:(aq % 2) * 512 + 512],
                                p_b[:, :], AF.Copy)
                            cx.inc(ins, "rbe", 1)

                @blk.vector
                def _(v):
                    npst = 0
                    for h in range(NH):
                        for qc in range(8):
                            aq = h * 8 + qc
                            nkt = kts(qc)
                            for kt in range(nkt):
                                if kt >= 4 * qc:
                                    cx.w(v, "pst", npst + 1)
                                    if npst >= 2:
                                        cx.w(v, "xps", npst - 1)
                                    ins = v.tensor_add(
                                        msk_s[:, (kt % 2) * 512:(kt % 2) * 512 + 512],
                                        p_s[kt % 2][:, :],
                                        mask_sb[:, (kt - 4 * qc) * 512:
                                                (kt - 4 * qc) * 512 + 512])
                                    cx.inc(ins, "mskv", 1)
                                npst += 1
                            cx.w(v, "pdn", aq + 1)
                            ins = v.reciprocal(rcf32[0:1, :], p_d[aq % 2][0:1, :])
                            cx.fence(v, ins, "fv")
                            ins = v.tensor_copy(
                                rc_sb[0:1, (aq % 2) * 512:(aq % 2) * 512 + 512],
                                rcf32[0:1, :])
                            cx.inc(ins, "rcv", 1)
                            cx.w(v, "rbe", aq + 1)
                            ins = v.tensor_mul(
                                attn_sb[:, h * S + qc * 512:h * S + qc * 512 + 512],
                                p_o[aq % 2][:, :],
                                rb_sb[:, (aq % 2) * 512:(aq % 2) * 512 + 512])
                            cx.inc(ins, "aat", 1)

        with ExitStack() as st3:
            sb3 = lambda nm, shp, dt: st3.enter_context(nc.sbuf_tensor(nm, shp, dt))
            wo_w = sb3("p3_wo", [128, 2 * KD * 128], BF16)
            s2row = sb3("p3_s2", [128, 2 * S], BF16)
            p_f = [st3.enter_context(nc.psum_tensor(f"p3_pf{i}", [128, 512], F32))
                   for i in range(2)]
            p_bt = st3.enter_context(nc.psum_tensor("p3_pbt", [128, 16], BF16))

            with nc.Block() as blk:
                @blk.tensor
                def _(t):
                    cx.w(t, "cl3g", cx.val("cl3g"))
                    cx.w(t, "id_v", cx.val("id_v"))
                    ins = t.transpose(p_bt[:, 0:16], bo_sb[:, :], ident[0:16, 0:16])
                    cx.inc(ins, "bt3", 1)

                @blk.vector
                def _(v):
                    cx.w(v, "bt3", 1)
                    cx.inc(v.tensor_copy(bo_pt[:, :], p_bt[:, 0:16]), "btv3", 1)

            with nc.Block() as blk:
                @blk.gpsimd
                def _(g):
                    for mi in range(16):
                        if mi >= 2:
                            cx.w(g, "s2e", 8 * (mi - 1))
                        cx.inc(g.dma_start(
                            out=wo_w[:, (mi % 2) * 2048:(mi % 2) * 2048 + 2048],
                            in_=_ap(T["wout1"], mi * 128,
                                    [[D, 128], [128 * D, KD], [1, 128]])),
                            "wol", 16)

                @blk.tensor
                def _(t):
                    cx.w(t, "btv3", 1)
                    for mi in range(16):
                        sl = (mi % 2) * 2048
                        cx.w(t, "wol", 16 * (mi + 1))
                        for tc in range(8):
                            g2 = mi * 8 + tc
                            if g2 >= 2:
                                cx.w(t, "s2e", g2 - 1)
                            for ki in range(KD):
                                ins = t.matmul(
                                    p_f[g2 % 2][:, :],
                                    wo_w[:, sl + ki * 128:sl + ki * 128 + 128],
                                    attn_sb[:, ki * S + tc * 512:ki * S + tc * 512 + 512],
                                    start=(ki == 0), stop=(ki == KD - 1))
                            cx.inc(ins, "s2p", 1)

                @blk.scalar
                def _(s):
                    for mi in range(16):
                        ssl = (mi % 2) * S
                        for tc in range(8):
                            g2 = mi * 8 + tc
                            cx.w(s, "s2p", g2 + 1)
                            if mi >= 2 and tc == 0:
                                cx.w(s, "os2", 16 * (mi - 1))
                            ins = s.activation(
                                s2row[:, ssl + tc * 512:ssl + tc * 512 + 512],
                                p_f[g2 % 2][:, :], AF.Identity,
                                bias=bo_pt[:, mi:mi + 1])
                            cx.inc(ins, "s2e", 1)

                @blk.sync
                def _(sy):
                    for mi in range(16):
                        ssl = (mi % 2) * S
                        cx.w(sy, "s2e", 8 * (mi + 1))
                        cx.inc(sy.dma_start(
                            out=_ap(T["g_s2"], mi * 128 * S, [[S, 128], [1, S]]),
                            in_=s2row[:, ssl:ssl + S]), "os2", 16)

            with nc.Block() as blk:
                @blk.sync
                def _(sy):
                    cx.wall(sy, "os2")
        if own_cx:
            _clear_sems(nc, cx)


def _p3_mskcnt(h, qc, kt):
    n = h * 32
    for q2 in range(qc):
        n += 4
    n += kt - 4 * qc
    return n


def _p3_pv_den(cx, t, h, qc, kt, nkt, aq, hsl, base_pst, v_sb, expT, p_o, p_d,
               ones_c):
    cx.w(t, "xps", base_pst + kt + 1)
    if kt == 0 and aq >= 2:
        cx.w(t, "aat", aq - 1)
    ins = t.matmul(p_o[aq % 2][:, :],
                   v_sb[:, hsl + kt * 128:hsl + kt * 128 + 128],
                   expT[:, (kt % 2) * 512:(kt % 2) * 512 + 512],
                   start=(kt == 0), stop=(kt == nkt - 1))
    cx.inc(ins, "pvp", 1)
    if kt == 0 and aq >= 2:
        cx.w(t, "rcv", aq - 1)
    ins = t.matmul(p_d[aq % 2][0:1, :], ones_c[:, 0:1],
                   expT[:, (kt % 2) * 512:(kt % 2) * 512 + 512],
                   start=(kt == 0), stop=(kt == nkt - 1))
    if kt == nkt - 1:
        cx.inc(ins, "pdn", 1)


def emit_p4(nc, T, cx=None):
    with ExitStack() as st:
        own_cx = cx is None
        cx = cx if cx is not None else Ctx(nc, st)
        sb = lambda nm, shp, dt: st.enter_context(nc.sbuf_tensor(nm, shp, dt))

        ident = sb("p4_id", [128, 128], BF16)
        iota_s = sb("p4_io", [128, 128], F32)
        ones_rb = sb("p4_onb", [1, 128], BF16)
        scl_sb = sb("p4_scl", [1, 4], F32)
        scl_hl = sb("p4_shl", [1, 4], BF16)
        scl_bc = sb("p4_sbc", [128, 1], F32)
        xn2c = sb("p4_xn", [128, KD * 512], BF16)
        s2c = sb("p4_s2", [128, KD * 512], BF16)
        h2 = sb("p4_h2", [128, NHF * 512], BF16)
        wfc_w = sb("p4_wf", [128, 2 * KD * 128], BF16)
        wpj_w = sb("p4_wp", [128, 2 * NHF * 128], BF16)
        rl_sb = sb("p4_rl", [128, 2 * 512], BF16)
        g2_sb = sb("p4_g2", [128, 2 * 512], BF16)
        sgt = sb("p4_sgt", [128, TS * D], BF16)
        x1t = sb("p4_x1", [128, 2 * D], F32)

        p_a = [st.enter_context(nc.psum_tensor(f"p4_pa{i}", [128, 512], F32))
               for i in range(2)]
        p_t = [st.enter_context(nc.psum_tensor(f"p4_pt{i}", [128, 128], BF16))
               for i in range(2)]
        p_b = st.enter_context(nc.psum_tensor("p4_pb", [128, 1], F32))

        _emit_ident(nc, cx, ident, iota_s)

        with nc.Block() as blk:
            @blk.gpsimd
            def _(g):
                g.memset(ones_rb[:, :], 1.0)

            @blk.sync
            def _(sy):
                cx.inc(sy.dma_start(out=scl_sb[0:1, 0:1], in_=T["g_scl"][0:1, 0:1]),
                       "cl4", 16)

        with nc.Block() as blk:
            @blk.scalar
            def _(s):
                cx.w(s, "cl4", 16)
                ins = s.activation(scl_hl[0:1, 0:1], scl_sb[0:1, 0:1], AF.Copy)
                cx.inc(ins, "shl4", 1)

            @blk.vector
            def _(v):
                cx.w(v, "shl4", 1)
                ins = v.tensor_copy(scl_sb[0:1, 1:2], scl_hl[0:1, 0:1])
                cx.fence(v, ins, "fv")
                ins = v.tensor_sub(scl_sb[0:1, 2:3], scl_sb[0:1, 0:1],
                                   scl_sb[0:1, 1:2])
                cx.fence(v, ins, "fv")
                ins = v.tensor_copy(scl_hl[0:1, 1:2], scl_sb[0:1, 2:3])
                cx.inc(ins, "shlv", 1)

            @blk.tensor
            def _(t):
                cx.w(t, "shlv", 1)
                t.matmul(p_b[:, 0:1], ones_rb[0:1, :], scl_hl[0:1, 0:1],
                         start=True, stop=False)
                ins = t.matmul(p_b[:, 0:1], ones_rb[0:1, :], scl_hl[0:1, 1:2],
                               start=False, stop=True)
                cx.inc(ins, "sb4", 1)

            @blk.vector
            def _(v):
                cx.w(v, "sb4", 1)
                cx.inc(v.tensor_copy(scl_bc[:, :], p_b[:, 0:1]), "sbv4", 1)

        for tc in range(8):
            bT = {k: cx.val(k) for k in
                  ("xl4", "wfl", "wpl", "pfc", "rle", "h2v", "ppj", "g2e",
                   "sgv4", "tp4", "te4", "x1l", "x2v", "oy", "id_v", "sbv4")}
            if DEV_DUMP and tc == 1:
                with nc.Block() as blk:
                    @blk.gpsimd
                    def _(g):
                        dbg = T["g_dbg"]
                        cx.inc(g.dma_start(
                            out=_ap(dbg, 0, [[NHF * 512, 128], [1, NHF * 512]]),
                            in_=h2[:, :]), "dbg4", 16)
                        cx.inc(g.dma_start(
                            out=_ap(dbg, 3000000, [[TS * D, 128], [1, TS * D]]),
                            in_=sgt[:, :]), "dbg4", 16)
                        cx.inc(g.dma_start(out=_ap(dbg, 3900000, [[1, 128], [1, 1]]),
                                           in_=scl_bc[:, :]), "dbg4", 16)
                        cx.inc(g.dma_start(out=_ap(dbg, 3910000, [[1024, 128], [1, 1024]]),
                                           in_=g2_sb[:, :]), "dbg4", 16)
                        g.wait_ge(cx.sem("dbg4"), cx.c["dbg4"])
            with nc.Block() as blk:
                @blk.sync
                def _(sy):
                    cx.w(sy, "oy", bT["oy"])
                    cx.inc(sy.dma_start(
                        out=xn2c[:, :],
                        in_=_ap(T["g_xn2"], tc * 512,
                                [[S, 128], [128 * S, KD], [1, 512]])), "xl4", 16)
                    cx.inc(sy.dma_start(
                        out=s2c[:, :],
                        in_=_ap(T["g_s2"], tc * 512,
                                [[S, 128], [128 * S, KD], [1, 512]])), "xl4", 16)
                    for mi in range(NHF):
                        if mi >= 2:
                            cx.w(sy, "h2v", bT["h2v"] + mi - 1)
                        cx.inc(sy.dma_start(
                            out=wfc_w[:, (mi % 2) * 2048:(mi % 2) * 2048 + 2048],
                            in_=_ap(T["g_q01fc"], mi * 128 * KD * 128,
                                    [[KD * 128, 128], [1, KD * 128]])), "wfl", 16)
                    for mi2 in range(16):
                        if mi2 >= 2:
                            cx.w(sy, "g2e", bT["g2e"] + mi2 - 1)
                        cx.inc(sy.dma_start(
                            out=wpj_w[:, (mi2 % 2) * 5120:(mi2 % 2) * 5120 + 5120],
                            in_=_ap(T["g_q01pj"], mi2 * 128 * NHF * 128,
                                    [[NHF * 128, 128], [1, NHF * 128]])), "wpl", 16)
                    for tsub in range(TS):
                        if tsub >= 2:
                            cx.w(sy, "x2v", bT["x2v"] + tsub - 1)
                            cx.w(sy, "oy", bT["oy"] + 16 * (tsub - 1))
                        cx.inc(sy.dma_start(
                            out=x1t[:, (tsub % 2) * D:(tsub % 2) * D + D],
                            in_=_ap(T["g_x1"], (tc * 512 + tsub * 128) * D,
                                    [[D, 128], [1, D]])), "x1l", 16)

                @blk.tensor
                def _(t):
                    cx.w(t, "xl4", bT["xl4"] + 32)
                    cx.w(t, "sbv4", 1)
                    cx.w(t, "id_v", cx.val("id_v"))
                    for mi in range(NHF):
                        sl = (mi % 2) * 2048
                        cx.w(t, "wfl", bT["wfl"] + 16 * (mi + 1))
                        if mi >= 2:
                            cx.w(t, "h2v", bT["h2v"] + mi - 1)
                        for ki in range(KD):
                            ins = t.matmul(
                                p_a[mi % 2][:, :],
                                wfc_w[:, sl + ki * 128:sl + ki * 128 + 128],
                                xn2c[:, ki * 512:ki * 512 + 512],
                                start=(ki == 0), stop=(ki == KD - 1))
                        cx.inc(ins, "pfc", 1)
                    for mi2 in range(16):
                        sl = (mi2 % 2) * 5120
                        cx.w(t, "wpl", bT["wpl"] + 16 * (mi2 + 1))
                        cx.w(t, "h2v", bT["h2v"] + NHF)
                        if mi2 >= 2:
                            cx.w(t, "g2e", bT["g2e"] + mi2 - 1)
                        for ki in range(NHF):
                            ins = t.matmul(
                                p_a[mi2 % 2][:, :],
                                wpj_w[:, sl + ki * 128:sl + ki * 128 + 128],
                                h2[:, ki * 512:ki * 512 + 512],
                                start=(ki == 0), stop=(ki == NHF - 1))
                        cx.inc(ins, "ppj", 1)
                        if mi2 >= 1:
                            _p4_tp(cx, t, bT, mi2 - 1, sgt, g2_sb, ident, p_t)
                    _p4_tp(cx, t, bT, 15, sgt, g2_sb, ident, p_t)

                @blk.scalar
                def _(s):
                    for mi in range(NHF):
                        cx.w(s, "pfc", bT["pfc"] + mi + 1)
                        if mi >= 2:
                            cx.w(s, "h2v", bT["h2v"] + mi - 1)
                        ins = s.activation(rl_sb[:, (mi % 2) * 512:(mi % 2) * 512 + 512],
                                           p_a[mi % 2][:, :], AF.Relu)
                        cx.inc(ins, "rle", 1)
                    for mi2 in range(16):
                        cx.w(s, "ppj", bT["ppj"] + mi2 + 1)
                        if mi2 >= 2:
                            cx.w(s, "sgv4", bT["sgv4"] + mi2 - 1)
                            cx.w(s, "tp4", bT["tp4"] + (mi2 - 1) * TS)
                        ins = s.activation(g2_sb[:, (mi2 % 2) * 512:(mi2 % 2) * 512 + 512],
                                           p_a[mi2 % 2][:, :], AF.Silu,
                                           scale=scl_bc[:, 0:1])
                        cx.inc(ins, "g2e", 1)

                @blk.vector
                def _(v):
                    for mi in range(NHF):
                        cx.w(v, "rle", bT["rle"] + mi + 1)
                        ins = v.tensor_mul(h2[:, mi * 512:mi * 512 + 512],
                                           p_a[mi % 2][:, :],
                                           rl_sb[:, (mi % 2) * 512:(mi % 2) * 512 + 512])
                        cx.inc(ins, "h2v", 1)
                    for mi2 in range(16):
                        cx.w(v, "g2e", bT["g2e"] + mi2 + 1)
                        ins = v.tensor_mul(g2_sb[:, (mi2 % 2) * 512:(mi2 % 2) * 512 + 512],
                                           g2_sb[:, (mi2 % 2) * 512:(mi2 % 2) * 512 + 512],
                                           s2c[:, mi2 * 512:mi2 * 512 + 512])
                        cx.inc(ins, "sgv4", 1)
                        if mi2 >= 1:
                            _p4_te(cx, v, bT, mi2 - 1, sgt, p_t)
                    _p4_te(cx, v, bT, 15, sgt, p_t)
                    for tsub in range(TS):
                        cx.w(v, "te4", bT["te4"] + 64)
                        cx.w(v, "x1l", bT["x1l"] + 16 * (tsub + 1))
                        if tsub >= 2:
                            cx.w(v, "oy", bT["oy"] + 16 * (tsub - 1))
                        ins = v.tensor_add(x1t[:, (tsub % 2) * D:(tsub % 2) * D + D],
                                           x1t[:, (tsub % 2) * D:(tsub % 2) * D + D],
                                           sgt[:, tsub * D:tsub * D + D])
                        cx.inc(ins, "x2v", 1)

                @blk.gpsimd
                def _(g):
                    for tsub in range(TS):
                        cx.w(g, "x2v", bT["x2v"] + tsub + 1)
                        cx.inc(g.dma_start(
                            out=_ap(T["y"], (tc * 512 + tsub * 128) * D,
                                    [[D, 128], [1, D]]),
                            in_=x1t[:, (tsub % 2) * D:(tsub % 2) * D + D]), "oy", 16)

        with nc.Block() as blk:
            @blk.gpsimd
            def _(g):
                cx.wall(g, "oy")
        if own_cx:
            _clear_sems(nc, cx)


def _p4_tp(cx, t, bT, mi2, sgt, g2_sb, ident, p_t):
    cx.w(t, "sgv4", bT["sgv4"] + mi2 + 1)
    for tsub in range(TS):
        n = mi2 * TS + tsub
        if n >= 2:
            cx.w(t, "te4", bT["te4"] + n - 1)
        ins = t.transpose(p_t[n % 2][:, :],
                          g2_sb[:, (mi2 % 2) * 512 + tsub * 128:
                                (mi2 % 2) * 512 + tsub * 128 + 128], ident[:, :])
        cx.inc(ins, "tp4", 1)


def _p4_te(cx, v, bT, mi2, sgt, p_t):
    for tsub in range(TS):
        n = mi2 * TS + tsub
        cx.w(v, "tp4", bT["tp4"] + n + 1)
        ins = v.tensor_copy(sgt[:, tsub * D + mi2 * 128:tsub * D + mi2 * 128 + 128],
                            p_t[n % 2][:, :])
        cx.inc(ins, "te4", 1)


def _mk_dram(nc, name, shape, dtype, kind):
    return nc.dram_tensor(name, shape, dtype, kind=kind).ap()


SPEC_IN = {
    "x": ([S, D], F32),
    "Wqkv0": ([D, F0], F32), "bqkv0": ([F0], F32),
    "Wout0": ([D, D], F32), "bout0": ([D], F32),
    "Wr": ([D, 8], F32),
    "Weg": ([E, D, DE], F32), "Weu": ([E, D, DE], F32), "Wed": ([E, DE, D], F32),
    "Wsg": ([D, DSH], F32), "Wsu": ([D, DSH], F32), "Wsd": ([DSH, D], F32),
    "wqkv1": ([D, 3 * D], F32), "b1": ([3 * D], F32),
    "wout1": ([D, D], F32), "bo1": ([D], F32),
    "wfc": ([HF, D], F32), "wproj": ([D, HF], F32),
    "cosb": ([ROT, S], F32), "sinb": ([ROT, S], F32),
    "pmat": ([ROT, ROT], F32), "mask4": ([128, 4 * 512], F32),
}
SPEC_SCR = {
    "g_mw": ([MW_TOTAL], BF16),
    "g_q01fc": ([Q01FC_TOTAL], BF16), "g_q01pj": ([Q01PJ_TOTAL], BF16),
    "g_scl": ([1, 4], F32),
    "g_wv": ([128 * KD * 128], BF16),
    "g_ws": ([128 * D], BF16),
    "g_cb": ([128, 16], F32),
    "g_tmp": ([8, 512], F32),
    "g_x1": ([S * D], F32), "g_xn2": ([D * S], BF16),
    "g_qk": ([2 * NH * 128 * S], BF16), "g_vtm": ([S * D], BF16),
    "g_s2": ([D * S], BF16),
    "g_dbg": ([1 << 23], F32),
}
SPEC_OUT = {"y": ([S, D], F32)}


def _run_fused(per_core, cores, inputs):
    nc = bass.Bass("TRN2", target_bir_lowering=False,
                   detect_race_conditions=False)
    T = {}
    for n in ALL_INS:
        T[n] = _mk_dram(nc, n, *SPEC_IN[n], kind="ExternalInput")
    for n, (shape, dt) in SPEC_SCR.items():
        T[n] = _mk_dram(nc, n, shape, dt, kind="Internal")
    for n, (shape, dt) in SPEC_OUT.items():
        T[n] = _mk_dram(nc, n, shape, dt, kind="ExternalOutput")
    emit_prep(nc, T)
    emit_p1(nc, T)
    emit_p2(nc, T)
    emit_p3(nc, T)
    emit_p4(nc, T)
    in_maps = [{k: per_core[b][k] for k in ALL_INS} for b in range(8)]
    r = run_bass_kernel_spmd(nc, in_maps, core_ids=cores)
    out = np.stack([r.results[b]["y"].reshape(S, D) for b in range(8)])
    return out.astype(np.float32)


def _build_phase(emit_fns, ins, scr_in, scr_out, outs):
    nc = bass.Bass("TRN2", target_bir_lowering=False, detect_race_conditions=False)
    T = {}
    for n in ins:
        T[n] = _mk_dram(nc, n, *SPEC_IN[n], kind="ExternalInput")
    for n, (shape, dt) in SPEC_SCR.items():
        if n in scr_in:
            T[n] = _mk_dram(nc, n, shape, dt, kind="ExternalInput")
        elif n in scr_out:
            T[n] = _mk_dram(nc, n, shape, dt, kind="ExternalOutput")
        else:
            T[n] = _mk_dram(nc, n, shape, dt, kind="Internal")
    for n, (shape, dt) in SPEC_OUT.items():
        if n in outs:
            T[n] = _mk_dram(nc, n, shape, dt, kind="ExternalOutput")
    for fn in emit_fns:
        fn(nc, T)
    return nc


def _np_inputs(inputs):
    f = np.ascontiguousarray
    base = {
        "Wqkv0": f(inputs["L0_attn_Wqkv"].reshape(D, F0)),
        "bqkv0": f(inputs["L0_attn_bqkv"].reshape(F0)),
        "Wout0": f(inputs["L0_attn_Wout"].reshape(D, D)),
        "bout0": f(inputs["L0_attn_bout"]),
        "Wr": f(inputs["L0_ffn_W_router"]),
        "Weg": f(inputs["L0_ffn_We_gate"]),
        "Weu": f(inputs["L0_ffn_We_up"]),
        "Wed": f(inputs["L0_ffn_We_down"]),
        "Wsg": f(inputs["L0_ffn_Ws_gate"]),
        "Wsu": f(inputs["L0_ffn_Ws_up"]),
        "Wsd": f(inputs["L0_ffn_Ws_down"]),
        "wqkv1": f(inputs["L1_attn_w_qkv"]),
        "b1": f(inputs["L1_attn_b_qkv"]),
        "wout1": f(inputs["L1_attn_w_out"]),
        "bo1": f(inputs["L1_attn_b_out"]),
        "wfc": f(inputs["L1_ffn_w_fc"]),
        "wproj": f(inputs["L1_ffn_w_proj"]),
    }
    inv = 1.0 / (10000.0 ** (np.arange(0, ROT, 2, dtype=np.float64) / ROT))
    tpos = np.arange(S, dtype=np.float64)
    fr = np.concatenate([np.outer(inv, tpos), np.outer(inv, tpos)], axis=0)
    base["cosb"] = np.cos(fr).astype(np.float32)
    base["sinb"] = np.sin(fr).astype(np.float32)
    pm = np.zeros((ROT, ROT), dtype=np.float32)
    for mrow in range(16):
        pm[mrow + 16, mrow] = -1.0
        pm[mrow, mrow + 16] = 1.0
    base["pmat"] = pm
    msk = np.zeros((128, 4 * 512), dtype=np.float32)
    for r in range(4):
        for i in range(128):
            msk[i, r * 512:(r + 1) * 512] = np.where(
                np.arange(512) >= r * 128 + i, 0.0, -1.0e9)
    base["mask4"] = msk
    per_core = []
    for b in range(8):
        m = dict(base)
        m["x"] = f(inputs["L0_attn_x"][b])
        per_core.append(m)
    return per_core


FUSED = True

ALL_INS = ["x", "Wqkv0", "bqkv0", "Wout0", "bout0", "Wr", "Weg", "Weu", "Wed",
           "Wsg", "Wsu", "Wsd", "wqkv1", "b1", "wout1", "bo1", "wfc", "wproj",
           "cosb", "sinb", "pmat", "mask4"]


def kernel(**inputs):
    per_core = _np_inputs(inputs)
    cores = list(range(8))
    if FUSED:
        return _run_fused(per_core, cores, inputs)

    nc = _build_phase([emit_prep],
                      ins=["Wqkv0", "Wout0", "Wr", "Weg", "Weu", "Wed", "Wsg",
                           "Wsu", "Wsd", "wfc", "wproj", "bqkv0", "bout0"],
                      scr_in=set(),
                      scr_out={"g_mw", "g_q01fc", "g_q01pj", "g_scl", "g_wv",
                               "g_ws", "g_cb"},
                      outs=set())
    in_maps = [{k: per_core[b][k] for k in
                ("Wqkv0", "Wout0", "Wr", "Weg", "Weu", "Wed", "Wsg", "Wsu",
                 "Wsd", "wfc", "wproj", "bqkv0", "bout0")} for b in range(8)]
    r1 = run_bass_kernel_spmd(nc, in_maps, core_ids=cores)

    nc = _build_phase([emit_p1],
                      ins=["x"],
                      scr_in={"g_mw", "g_wv", "g_ws", "g_cb"},
                      scr_out={"g_x1", "g_xn2"}, outs=set())
    in_maps = []
    for b in range(8):
        m = {"x": per_core[b]["x"]}
        for kk in ("g_mw", "g_wv", "g_ws", "g_cb"):
            m[kk] = r1.results[b][kk]
        in_maps.append(m)
    r2 = run_bass_kernel_spmd(nc, in_maps, core_ids=cores)

    nc = _build_phase([emit_p2],
                      ins=["wqkv1", "b1", "cosb", "sinb", "pmat"],
                      scr_in={"g_xn2"}, scr_out={"g_qk", "g_vtm"}, outs=set())
    in_maps = []
    for b in range(8):
        m = {k: per_core[b][k] for k in ("wqkv1", "b1", "cosb", "sinb", "pmat")}
        m["g_xn2"] = r2.results[b]["g_xn2"]
        in_maps.append(m)
    r3 = run_bass_kernel_spmd(nc, in_maps, core_ids=cores)

    nc = _build_phase([emit_p3],
                      ins=["wout1", "bo1", "mask4"],
                      scr_in={"g_qk", "g_vtm"}, scr_out={"g_s2"}, outs=set())
    in_maps = []
    for b in range(8):
        m = {k: per_core[b][k] for k in ("wout1", "bo1", "mask4")}
        m["g_qk"] = r3.results[b]["g_qk"]
        m["g_vtm"] = r3.results[b]["g_vtm"]
        in_maps.append(m)
    r4 = run_bass_kernel_spmd(nc, in_maps, core_ids=cores)

    nc = _build_phase([emit_p4], ins=[],
                      scr_in={"g_xn2", "g_x1", "g_s2", "g_q01fc", "g_q01pj",
                              "g_scl"},
                      scr_out=set(), outs={"y"})
    in_maps = []
    for b in range(8):
        m = {"g_xn2": r2.results[b]["g_xn2"], "g_x1": r2.results[b]["g_x1"],
             "g_s2": r4.results[b]["g_s2"], "g_q01fc": r1.results[b]["g_q01fc"],
             "g_q01pj": r1.results[b]["g_q01pj"], "g_scl": r1.results[b]["g_scl"]}
        in_maps.append(m)
    r5 = run_bass_kernel_spmd(nc, in_maps, core_ids=cores)

    out = np.stack([r5.results[b]["y"].reshape(S, D) for b in range(8)])
    return out.astype(np.float32)
